# version 12
# baseline (speedup 1.0000x reference)
from contextlib import ExitStack

import numpy as np
import concourse.bass as bass
import concourse.mybir as mybir
from concourse.bass_utils import run_bass_kernel_spmd

F32 = mybir.dt.float32
F16 = mybir.dt.float16
ALU = mybir.AluOpType
ACTF = mybir.ActivationFunctionType
AX = mybir.AxisListType.X

B, S, D = 4, 2048, 2048
NH, DH, ROT = 32, 64, 32
HFF = 5632
NCORE = 8
TOK_OWN = S // 2
NT_OWN = TOK_OWN // 128
NT_ALL = S // 128
ND = D // 128
NG = 4
GH = 8
NHC = HFF // 128
YBLK = 4

DBG_GROUPS = NG
DBG_UNITS = 10 ** 9
DBG_TILES = NT_ALL
DBG_SERIAL = False
DBG_P5TILES = NT_OWN
DBG_NOROPE = False
DBG_NOTRANS = False
DBG_ROPLVL = 5

C_EXPSC, C_EPS, C_NHALF, C_INVD, C_THRSC, C_THRB, C_ONE, C_NEG1, C_ZERO = range(9)
NCONST = 16

ENGINES = ("sync", "gpsimd", "vector", "scalar", "tensor")


class Sem:
    def __init__(self, nc, name):
        self.h = nc.alloc_semaphore(name)
        self.v = 0


class Plan:

    def __init__(self):
        self.ops = {e: [] for e in ENGINES}

    def emit(self, eng, fn, wait=(), inc=()):
        self.ops[eng].append((list(wait), fn, list(inc)))
        for s, n in inc:
            s.v += n

    def pour(self, nc, name):
        with nc.Block(name) as blk:
            for ename in ENGINES:
                if not self.ops[ename]:
                    continue

                def body(e, ename=ename):
                    for waits, fn, incs in self.ops[ename]:
                        for s, v in waits:
                            e.wait_ge(s.h, v)
                        ins = fn(e)
                        for s, n in incs:
                            ins.then_inc(s.h, n)

                getattr(blk, ename)(body)


def _hc(ap_t, c):
    return ap_t.rearrange("p (h c) -> p h c", c=c)


def build(dbg=None):
    nc = bass.Bass("TRN2", target_bir_lowering=False, debug=False,
                   detect_race_conditions=False)

    xin = nc.dram_tensor("xin", [S, D], F32, kind="ExternalInput").ap()
    wqkv = nc.dram_tensor("wqkv", [D, 3 * D], F32, kind="ExternalInput").ap()
    bqkv = nc.dram_tensor("bqkv", [1, 3 * D], F32, kind="ExternalInput").ap()
    wout_in = nc.dram_tensor("wout", [D, D], F32, kind="ExternalInput").ap()
    bout = nc.dram_tensor("bout", [1, D], F32, kind="ExternalInput").ap()
    w1 = nc.dram_tensor("w1", [HFF, D], F32, kind="ExternalInput").ap()
    w2 = nc.dram_tensor("w2", [HFF, D], F32, kind="ExternalInput").ap()
    w3 = nc.dram_tensor("w3", [D, HFF], F32, kind="ExternalInput").ap()
    c1_in = nc.dram_tensor("c1", [S, 256], F16, kind="ExternalInput").ap()
    c2_in = nc.dram_tensor("c2", [S, 256], F16, kind="ExternalInput").ap()
    btbl_in = nc.dram_tensor("btbl", [128, 32], F32, kind="ExternalInput").ap()
    mband_in = nc.dram_tensor("mband", [128, 896], F16, kind="ExternalInput").ap()
    ident_in = nc.dram_tensor("ident", [128, 128], F16, kind="ExternalInput").ap()
    consts_in = nc.dram_tensor("consts", [128, NCONST], F32, kind="ExternalInput").ap()

    yout = nc.dram_tensor("yout", [TOK_OWN, D], F32, kind="ExternalOutput").ap()

    w1f = nc.dram_tensor("w1f", [HFF, D], F16, kind="Internal").ap()
    w2f = nc.dram_tensor("w2f", [HFF, D], F16, kind="Internal").ap()
    w3f = nc.dram_tensor("w3f", [D, HFF], F16, kind="Internal").ap()
    thr_hbm = nc.dram_tensor("thr_hbm", [1, 8], F32, kind="Internal").ap()

    dbg_outs = {}
    if dbg == "p1":
        dbg_outs["ln1"] = nc.dram_tensor("dbg_ln1", [ND * 128, S], F16, kind="ExternalOutput").ap()
    if dbg == "thr":
        dbg_outs["thr"] = nc.dram_tensor("dbg_thr", [128, 8], F32, kind="ExternalOutput").ap()
    if dbg == "p2":
        dbg_outs["qfm"] = nc.dram_tensor("dbg_qfm", [4 * 128, TOK_OWN], F16, kind="ExternalOutput").ap()
        dbg_outs["kfm"] = nc.dram_tensor("dbg_kfm", [4 * 128, S], F16, kind="ExternalOutput").ap()
        dbg_outs["vtm"] = nc.dram_tensor("dbg_vtm", [NT_ALL * 128, GH * 65], F16, kind="ExternalOutput").ap()
    if dbg == "p3":
        dbg_outs["aout"] = nc.dram_tensor("dbg_aout", [ND * 128, TOK_OWN], F16, kind="ExternalOutput").ap()
    if dbg == "p3u":
        dbg_outs["at"] = nc.dram_tensor("dbg_at", [2 * 128, 512], F16, kind="ExternalOutput").ap()
        dbg_outs["srow"] = nc.dram_tensor("dbg_srow", [1, 512], F32, kind="ExternalOutput").ap()
        dbg_outs["bc"] = nc.dram_tensor("dbg_bc", [64, 512], F32, kind="ExternalOutput").ap()
        dbg_outs["ao0"] = nc.dram_tensor("dbg_ao0", [128, TOK_OWN], F16, kind="ExternalOutput").ap()
    if dbg == "p5":
        dbg_outs["x2"] = nc.dram_tensor("dbg_x2", [TOK_OWN, D], F16, kind="ExternalOutput").ap()
        dbg_outs["ln3"] = nc.dram_tensor("dbg_ln3", [ND * 128, TOK_OWN], F16, kind="ExternalOutput").ap()
    if dbg == "p6":
        dbg_outs["y"] = nc.dram_tensor("dbg_y", [TOK_OWN, D], F16, kind="ExternalOutput").ap()
    if dbg == "fd":
        dbg_outs["kf0"] = nc.dram_tensor("dbg_kf0", [128, S], F16, kind="ExternalOutput").ap()
        dbg_outs["qf0"] = nc.dram_tensor("dbg_qf0", [128, TOK_OWN], F16, kind="ExternalOutput").ap()
        dbg_outs["vt0"] = nc.dram_tensor("dbg_vt0", [128, GH * 65], F16, kind="ExternalOutput").ap()
        dbg_outs["l10"] = nc.dram_tensor("dbg_l10", [128, S], F16, kind="ExternalOutput").ap()
        dbg_outs["l1t"] = nc.dram_tensor("dbg_l1t", [128, D], F16, kind="ExternalOutput").ap()
        dbg_outs["ao"] = nc.dram_tensor("dbg_ao", [ND * 128, TOK_OWN], F16, kind="ExternalOutput").ap()
        dbg_outs["x2d"] = nc.dram_tensor("dbg_x2d", [TOK_OWN, D], F16, kind="ExternalOutput").ap()
        dbg_outs["yd"] = nc.dram_tensor("dbg_yd", [TOK_OWN, D], F16, kind="ExternalOutput").ap()
        dbg_outs["y1s"] = nc.dram_tensor("dbg_y1s", [128, D], F32, kind="ExternalOutput").ap()
        dbg_outs["sts"] = nc.dram_tensor("dbg_sts", [128, 16], F32, kind="ExternalOutput").ap()
    if dbg == "p5a":
        dbg_outs["y1"] = nc.dram_tensor("dbg_y1", [128, D], F32, kind="ExternalOutput").ap()
        dbg_outs["st"] = nc.dram_tensor("dbg_st", [128, 16], F32, kind="ExternalOutput").ap()
        dbg_outs["x20"] = nc.dram_tensor("dbg_x20", [128, D], F16, kind="ExternalOutput").ap()
        dbg_outs["ao0"] = nc.dram_tensor("dbg_ao0", [128, TOK_OWN], F16, kind="ExternalOutput").ap()
        dbg_outs["wo0"] = nc.dram_tensor("dbg_wo0", [128, D], F16, kind="ExternalOutput").ap()

    A = nc.alloc_sbuf_tensor

    consts = A("consts_sb", [128, NCONST], F32)
    btbl = A("btbl_sb", [128, 32], F32)
    mband = A("mband_sb", [128, 896], F16)
    ident = A("ident_sb", [128, 128], F16)
    onesr = A("onesr", [1, 128], F16)
    onescol = A("onescol", [128, 1], F16)
    gacc = A("gacc", [128, 288], F32)
    gcol = A("gcol", [128, 4], F32)
    gcol16 = A("gcol16", [128, 4], F16)
    grow = A("grow", [1, 8], F32)
    thr = A("thr_sb", [128, 8], F32)
    stat = A("stat", [128, 32], F32)
    srow = A("srow", [1, 512], F32)
    rrow = A("rrow", [33, 512], F16)
    bcrow = A("bcrow", [64, 512], F16)

    aout = [A(f"aout{i}", [128, TOK_OWN], F16) for i in range(ND)]
    ln3_fm = aout
    ln1 = [A(f"ln1_{i}", [128, S], F16) for i in range(ND)]
    wout_sb = ln1

    pstk = ExitStack()
    pb = [pstk.enter_context(nc.psum_tensor(f"pb{i}", [128, 512], F32)) for i in range(6)]
    ptr = pstk.enter_context(nc.psum_tensor("ptr", [128, 512], F16))
    pbc = pstk.enter_context(nc.psum_tensor("pbc", [64, 512], F32))

    s_ld = Sem(nc, "s_ld")
    s_xp = [Sem(nc, "s_x0"), Sem(nc, "s_x1")]
    s_cp = [Sem(nc, "s_c0"), Sem(nc, "s_c1")]
    s_wk = Sem(nc, "s_wk")
    s_gp = [Sem(nc, f"s_g{i}") for i in range(4)]
    s_gbp = [Sem(nc, f"s_gb{i}") for i in range(4)]
    s_gr = Sem(nc, "s_gr")
    s_gv2 = Sem(nc, "s_gv2")
    s_mm = Sem(nc, "s_mm")
    s_ve = Sem(nc, "s_ve")
    s_se = Sem(nc, "s_se")
    s_stp = [Sem(nc, "s_st0"), Sem(nc, "s_st1")]
    s_fw12 = [Sem(nc, "s_fw12a"), Sem(nc, "s_fw12b")]
    s_fw3 = [Sem(nc, "s_fw3a"), Sem(nc, "s_fw3b")]
    s_thr = Sem(nc, "s_thr")
    s_dbg = Sem(nc, "s_dbg")

    p = Plan()
    p.emit("sync", lambda e: e.dma_start(out=consts[:, :], in_=consts_in[:, :]), inc=[(s_ld, 16)])
    p.emit("sync", lambda e: e.dma_start(out=btbl[:, :], in_=btbl_in[:, :]), inc=[(s_ld, 16)])
    p.emit("sync", lambda e: e.dma_start(out=ident[:, :], in_=ident_in[:, :]), inc=[(s_ld, 16)])
    p.emit("sync", lambda e: e.dma_start(out=mband[:, :], in_=mband_in[:, :]), inc=[(s_ld, 16)])
    p.emit("vector", lambda e: e.memset(onesr[:, :], 1.0), inc=[(s_ve, 1)])
    p.emit("vector", lambda e: e.memset(onescol[:, :], 1.0), inc=[(s_ve, 1)])
    p.emit("vector", lambda e: e.memset(grow[:, :], 0.0), inc=[(s_ve, 1)])
    p.emit("vector", lambda e: e.memset(stat[:, :], 0.0), inc=[(s_ve, 1)])
    p.pour(nc, "p0")
    VE0 = s_ve.v

    gld, gbd = {}, {}
    gchunks = []
    for wi, (src, dst) in enumerate(((w1, w1f), (w2, w2f), (w3, w3f))):
        ncols = src.shape[1]
        for ch in range(src.shape[0] // 128):
            for c0 in range(0, ncols, 1024):
                gchunks.append((wi, src, dst, ch, c0, min(1024, ncols - c0)))
    NGC = len(gchunks)
    GOFFS = []
    for wi in range(3):
        idx = [i for i, gc in enumerate(gchunks) if gc[0] == wi]
        GOFFS.append((min(idx), len(idx)))
    gst = {"ld": 0}

    def gamma_pump(p, n, gbuf):
        def red(i):
            wi, src, dst, ch, c0, cn = gchunks[i]
            buf = gbuf[i % 3]
            sg = s_gp[i % 3]
            p.emit("sync",
                   lambda e, buf=buf, dst=dst, ch=ch, c0=c0, cn=cn:
                       e.dma_start(out=dst[128 * ch:128 * ch + 128, c0:c0 + cn], in_=buf[:, 0:cn]),
                   wait=[(sg, gld[i])], inc=[(s_gbp[i % 3], 16)])
            gbd[i] = s_gbp[i % 3].v
            p.emit("vector",
                   lambda e, buf=buf, i=i, cn=cn:
                       e.reduce_sum(gacc[:, i:i + 1], buf[:, 0:cn], axis=AX, apply_absolute_value=True),
                   wait=[(sg, gld[i])], inc=[(s_gr, 1)])

        end = min(gst["ld"] + n, NGC)
        while gst["ld"] < end:
            i = gst["ld"]
            wi, src, dst, ch, c0, cn = gchunks[i]
            buf = gbuf[i % 3]
            sg = s_gp[i % 3]
            w = [(s_gr, i - 2), (s_gbp[i % 3], gbd[i - 3])] if i >= 3 else []
            p.emit("gpsimd",
                   lambda e, buf=buf, src=src, ch=ch, c0=c0, cn=cn:
                       e.dma_start(out=buf[:, 0:cn], in_=src[128 * ch:128 * ch + 128, c0:c0 + cn]),
                   wait=w, inc=[(sg, 16)])
            gld[i] = sg.v
            if i >= 2:
                red(i - 2)
            gst["ld"] = i + 1
            if i == NGC - 1:
                red(NGC - 2)
                red(NGC - 1)

    def gamma_finish(p):
        offs = GOFFS

        def red(e):
            ins = None
            for wi2, (o, nn) in enumerate(offs):
                ins = e.reduce_sum(gcol[:, wi2:wi2 + 1], gacc[:, o:o + nn], axis=AX)
            return ins
        p.emit("vector", red, wait=[(s_gr, NGC)], inc=[(s_ve, 1)])
        p.emit("vector", lambda e: e.tensor_scalar_mul(gcol16[:, 0:3], gcol[:, 0:3], 1.0 / 65536.0),
               wait=[(s_ve, s_ve.v)], inc=[(s_ve, 1)])
        p.emit("tensor", lambda e: e.matmul(pb[5][0:1, 0:3], onescol[:, 0:1], gcol16[:, 0:3],
                                            start=True, stop=True),
               wait=[(s_ve, s_ve.v)], inc=[(s_mm, 1)])
        p.emit("vector", lambda e: e.tensor_scalar(out=grow[0:1, 0:3], in0=pb[5][0:1, 0:3],
                                                   scalar1=65536.0 * 0.5 / (HFF * D), scalar2=0.5e-8,
                                                   op0=ALU.mult, op1=ALU.add),
               wait=[(s_mm, s_mm.v)], inc=[(s_ve, 1)])
        p.emit("vector", lambda e: e.tensor_scalar_mul(grow[0:1, 4:7], grow[0:1, 0:3], -1.0),
               wait=[(s_ve, s_ve.v)], inc=[(s_ve, 1)])
        p.emit("sync", lambda e: e.dma_start(out=thr_hbm[0:1, 0:8], in_=grow[0:1, 0:8]),
               wait=[(s_ve, s_ve.v)], inc=[(s_gbp[0], 16)])
        p.emit("sync", lambda e: e.dma_start(out=thr[:, 0:8], in_=thr_hbm[0, 0:8].partition_broadcast(128)),
               wait=[(s_gbp[0], s_gbp[0].v)], inc=[(s_thr, 16)])

    stkg = ExitStack()
    gbuf = [stkg.enter_context(nc.sbuf_tensor(f"gbuf{i}", [128, 1024], F16)) for i in range(3)]
    stk1 = ExitStack()
    E1 = lambda name, shape, dt: stk1.enter_context(nc.sbuf_tensor(name, shape, dt))
    ln1tm = [E1(f"ln1tm{i}", [128, D], F16) for i in range(2)]
    xbuf = [E1(f"xbuf{i}", [128, D], F32) for i in range(2)]
    scr0 = E1("p1scr", [128, 2048], F32)
    scr1p = [E1(f"p1scrb{i}", [128, 1024], F32) for i in range(2)]

    vstamp, mmstamp, sqred, normst = {}, {}, {}, {}
    sestamp = {"tr": 0}

    p = Plan()
    p.emit("vector", lambda e: e.memset(scr0[:, :], 0.0), inc=[(s_ve, 1)])
    p.emit("vector", lambda e: e.memset(scr1p[0][:, :], 0.0), inc=[(s_ve, 1)])
    p.emit("vector", lambda e: e.memset(scr1p[1][:, :], 0.0), inc=[(s_ve, 1)])
    MEMS0 = s_ve.v

    def p1_s1(tt):
        sb = 8 * (tt % 2)
        xb = xbuf[tt % 2]
        w = [(s_ve, vstamp[tt - 2])] if tt >= 2 else []
        sx = s_xp[tt % 2]
        p.emit("sync", lambda e, xb=xb, tt=tt: e.dma_start(out=xb[:, :], in_=xin[128 * tt:128 * tt + 128, :]),
               wait=w, inc=[(sx, 16)])
        wv = [(sx, sx.v)]
        if tt == 0:
            wv.append((s_ld, s_ld.v))
        p.emit("vector", lambda e, xb=xb, sb=sb: e.reduce_sum(stat[:, sb:sb + 1], xb[:, :], axis=AX), wait=wv)
        wsq = [(sx, sx.v)]
        if tt >= 2:
            wsq.append((s_ve, sqred[tt - 2]))
        else:
            wsq.append((s_ve, MEMS0))
        p.emit("scalar", lambda e, xb=xb, tt=tt: e.square(scr0[:, 1024 * (tt % 2):1024 * (tt % 2) + 1024],
                                                          xb[:, 0:1024]),
               wait=wsq, inc=[(s_se, 1)])
        p.emit("scalar", lambda e, xb=xb, tt=tt: e.square(scr1p[tt % 2][:, :], xb[:, 1024:2048]),
               inc=[(s_se, 1)])
        p.emit("vector", lambda e, sb=sb, tt=tt: e.reduce_sum(stat[:, sb + 1:sb + 2],
                                                              scr0[:, 1024 * (tt % 2):1024 * (tt % 2) + 1024],
                                                              axis=AX),
               wait=[(s_se, s_se.v)], inc=[(s_ve, 1)])
        p.emit("vector", lambda e, sb=sb, tt=tt: e.reduce_sum(stat[:, sb + 7:sb + 8], scr1p[tt % 2][:, :], axis=AX),
               wait=[(s_ve, s_ve.v)], inc=[(s_ve, 1)])
        sqred[tt] = s_ve.v
        p.emit("vector", lambda e, sb=sb: e.tensor_add(stat[:, sb + 1:sb + 2], stat[:, sb + 1:sb + 2],
                                                       stat[:, sb + 7:sb + 8]),
               wait=[(s_ve, s_ve.v)], inc=[(s_ve, 1)])
        p.emit("vector", lambda e, sb=sb: e.tensor_scalar_mul(stat[:, sb + 2:sb + 3], stat[:, sb:sb + 1], 1.0 / D),
               wait=[(s_ve, s_ve.v)], inc=[(s_ve, 1)])
        p.emit("vector", lambda e, sb=sb: e.tensor_scalar_mul(stat[:, sb + 3:sb + 4], stat[:, sb + 1:sb + 2], 1.0 / D),
               wait=[(s_ve, s_ve.v)], inc=[(s_ve, 1)])
        p.emit("vector", lambda e, sb=sb: e.scalar_tensor_tensor(out=stat[:, sb + 4:sb + 5], in0=stat[:, sb + 2:sb + 3],
                                                                 scalar=1.0, in1=stat[:, sb + 2:sb + 3],
                                                                 op0=ALU.mult, op1=ALU.mult),
               wait=[(s_ve, s_ve.v)], inc=[(s_ve, 1)])
        p.emit("vector", lambda e, sb=sb: e.tensor_sub(stat[:, sb + 5:sb + 6], stat[:, sb + 3:sb + 4],
                                                       stat[:, sb + 4:sb + 5]),
               wait=[(s_ve, s_ve.v)], inc=[(s_ve, 1)])
        p.emit("scalar", lambda e, sb=sb: e.activation(out=stat[:, sb + 7:sb + 8], in_=stat[:, sb + 5:sb + 6],
                                                       func=ACTF.Ln, bias=consts[:, C_EPS:C_EPS + 1],
                                                       scale=consts[:, C_ONE:C_ONE + 1]),
               wait=[(s_ve, s_ve.v)], inc=[(s_se, 1)])
        p.emit("scalar", lambda e, sb=sb: e.activation(out=stat[:, sb + 6:sb + 7], in_=stat[:, sb + 7:sb + 8],
                                                       func=ACTF.Exp, bias=consts[:, C_ZERO:C_ZERO + 1],
                                                       scale=consts[:, C_NHALF:C_NHALF + 1]),
               wait=[(s_se, s_se.v)], inc=[(s_se, 1)])
        rstamp[tt] = s_se.v

    def p1_s2(tt):
        sb = 8 * (tt % 2)
        xb = xbuf[tt % 2]
        lt = ln1tm[tt % 2]
        wapply = [(s_se, rstamp[tt])]
        if tt >= 2:
            wapply.append((s_mm, mmstamp[tt - 2]))
        p.emit("vector", lambda e, xb=xb, lt=lt, sb=sb: e.tensor_scalar(
            out=lt[:, :], in0=xb[:, :], scalar1=stat[:, sb + 2:sb + 3], scalar2=stat[:, sb + 6:sb + 7],
            op0=ALU.subtract, op1=ALU.mult),
               wait=wapply, inc=[(s_ve, 1)])
        vstamp[tt] = s_ve.v
        for fq in range(4):
            def trans(e, lt=lt, fq=fq):
                ins = None
                for i in range(4):
                    dc = 4 * fq + i
                    ins = e.transpose(ptr[:, 128 * i:128 * i + 128],
                                      lt[:, 128 * dc:128 * dc + 128], ident[:, :])
                return ins
            wlist = [(s_ve, s_ve.v)]
            if (tt, fq) == (0, 0):
                wlist.append((s_ld, s_ld.v))
            else:
                wlist.append((s_se, sestamp["tr"]))
            p.emit("tensor", trans, wait=wlist, inc=[(s_mm, 1)])

            def evac(e, tt=tt, fq=fq):
                ins = None
                for i in range(4):
                    dc = 4 * fq + i
                    ins = e.copy(ln1[dc][:, 128 * tt:128 * tt + 128], ptr[:, 128 * i:128 * i + 128])
                return ins
            p.emit("scalar", evac, wait=[(s_mm, s_mm.v)], inc=[(s_se, 1)])
            sestamp["tr"] = s_se.v
        mmstamp[tt] = s_mm.v

    rstamp = {}
    for tt in range(NT_ALL + 1):
        if tt < NT_ALL:
            p1_s1(tt)
        if tt >= 1:
            p1_s2(tt - 1)
        gamma_pump(p, 1, gbuf)
    if dbg == "thr":
        gamma_pump(p, NGC, gbuf)
        gamma_finish(p)
    p.pour(nc, "p1")

    if dbg in ("p1", "thr"):
        p = Plan()
        if dbg == "p1":
            for dc in range(ND):
                p.emit("sync", lambda e, dc=dc: e.dma_start(
                    out=dbg_outs["ln1"][128 * dc:128 * dc + 128, :], in_=ln1[dc][:, :]), inc=[(s_dbg, 16)])
        else:
            p.emit("sync", lambda e: e.dma_start(out=dbg_outs["thr"][:, :], in_=thr[:, :]),
                   wait=[(s_thr, 16)], inc=[(s_dbg, 16)])
        p.emit("sync", lambda e: e.nop(), wait=[(s_dbg, s_dbg.v)])
        p.pour(nc, "dbgf")
        return nc

    stk1.close()

    stk23 = ExitStack()
    E23 = lambda name, shape, dt: stk23.enter_context(nc.sbuf_tensor(name, shape, dt))
    wqres = [E23(f"wqres{i}", [128, 1536], F16) for i in range(ND)]
    cbuf = [E23(f"cbuf{i}", [128, 512], F16) for i in range(2)]
    qfm = [E23(f"qfm{i}", [128, TOK_OWN], F16) for i in range(4)]
    kfm = [E23(f"kfm{i}", [128, S], F16) for i in range(4)]
    vtm = [E23(f"vtm{i}", [128, GH * 65], F16) for i in range(NT_ALL)]
    attnT = [E23(f"attnT{i}", [128, 512], F16) for i in range(3)]
    qk_tm = [attnT[0], attnT[1]]
    bqg = E23("bqg", [1, 3 * 512], F16)
    rsa = [E23("rsa0", [128, 256], F16)] * 2
    rsb = [E23("rsb0", [128, 256], F16)] * 2

    p = Plan()

    def _memset_vtm(e):
        ins = None
        for i in range(NT_ALL):
            ins = e.memset(vtm[i][:, :], 1.0)
        return ins
    p.emit("vector", _memset_vtm, inc=[(s_ve, 1)])
    p.pour(nc, "vns")
    VTM_VE = s_ve.v

    last_qkv_mm = {(-1): 0}
    wq_loaded = {}

    for g in range(min(NG, max(DBG_GROUPS, 1))):
        p = Plan()
        if g == 0:
            for dc in range(ND):
                p.emit("gpsimd", lambda e, dc=dc: e.dma_start(
                    out=wqres[dc][:, :], in_=wqkv[128 * dc:128 * dc + 128, 0:1536]),
                       inc=[(s_wk, 16)])
            p.emit("gpsimd", lambda e: e.dma_start(out=bqg[0:1, :], in_=bqkv[0:1, 0:1536]),
                   inc=[(s_wk, 16)])
            wq_loaded[0] = s_wk.v

        wk_all = wq_loaded[g]
        evst = {}
        qk_trans = {}

        cb_free = {}
        cst = {}
        mmq = {}

        def p2_s1(t):
            subs = (0, 1, 2) if t < NT_OWN else (1, 2)
            banks = [pb[3 * (t % 2) + i] for i in range(3)]
            wcb = [(s_ve, cb_free[t - 2])] if t >= 2 else []
            sc = s_cp[t % 2]
            p.emit("sync", lambda e, t=t: e.dma_start(out=cbuf[t % 2][:, 0:256],
                                                      in_=c1_in[128 * t:128 * t + 128, :]),
                   wait=wcb, inc=[(sc, 16)])
            p.emit("sync", lambda e, t=t: e.dma_start(out=cbuf[t % 2][:, 256:512],
                                                      in_=c2_in[128 * t:128 * t + 128, :]),
                   inc=[(sc, 16)])
            cst[t] = (sc, sc.v)
            wmm = [(s_wk, wk_all)]
            if t >= 2:
                wmm += [(s_ve, evst[t - 2][0]), (s_se, evst[t - 2][1])]
            if g == 0 and t == 0:
                wmm.append((s_ve, VTM_VE))

            def qkv_mms(e, t=t, subs=subs, banks=banks):
                ins = None
                for dc in range(ND):
                    for sub in subs:
                        ins = e.matmul(banks[sub][:, 0:512], ln1[dc][:, 128 * t:128 * t + 128],
                                       wqres[dc][:, 512 * sub:512 * sub + 512],
                                       start=(dc == 0), stop=False)
                for sub in subs:
                    ins = e.matmul(banks[sub][:, 0:512], onesr[0:1, 0:128],
                                   bqg[0:1, 512 * sub:512 * sub + 512], start=False, stop=True)
                return ins
            p.emit("tensor", qkv_mms, wait=wmm, inc=[(s_mm, 1)])
            mmq[t] = s_mm.v

        def p2_s2(t):
            subs = (0, 1, 2) if t < NT_OWN else (1, 2)
            banks = [pb[3 * (t % 2) + i] for i in range(3)]
            mm_qkv = mmq[t]
            c_sem, c_stamp = cst[t]
            p.emit("vector", lambda e, t=t, bk=banks[2]: e.tensor_copy(
                _hc(vtm[t][:, :], 65)[:, :, 0:64], _hc(bk[:, 0:512], 64)),
                   wait=[(s_mm, mm_qkv)], inc=[(s_ve, 1)])
            v_ve = s_ve.v
            ve_rope_last = 0
            for sub in subs[:-1]:
                bk = banks[sub]
                sa, sb2 = rsa[sub], rsb[sub]
                qt = qk_tm[t % 2]
                wr0 = [(s_mm, mm_qkv), (c_sem, c_stamp)]
                if sub == 1 and 0 in subs:
                    wr0.append((s_mm, qk_trans[(t, 0)]))
                elif t >= 2:
                    wr0.append((s_mm, qk_trans[(t - 2, (0, 1, 2)[-2] if t - 2 < NT_OWN else 1)]))
                p.emit("vector", lambda e, bk=bk, sa=sa, t=t: e.tensor_mul(
                    _hc(sa, 32)[:, :, 0:16], _hc(bk[:, 0:512], 64)[:, :, 16:32],
                    _hc(cbuf[t % 2][:, 256:512], 32)[:, :, 0:16]),
                       wait=wr0)
                p.emit("vector", lambda e, bk=bk, sa=sa, t=t: e.tensor_mul(
                    _hc(sa, 32)[:, :, 16:32], _hc(bk[:, 0:512], 64)[:, :, 0:16],
                    _hc(cbuf[t % 2][:, 256:512], 32)[:, :, 16:32]))
                p.emit("vector", lambda e, bk=bk, sb2=sb2, t=t: e.tensor_mul(
                    _hc(sb2, 32), _hc(bk[:, 0:512], 64)[:, :, 0:32],
                    _hc(cbuf[t % 2][:, 0:256], 32)),
                       inc=[(s_ve, 1)])
                p.emit("vector", lambda e, sa=sa, sb2=sb2, qt=qt: e.tensor_add(
                    _hc(qt, 64)[:, :, 0:32], _hc(sb2, 32), _hc(sa, 32)),
                       inc=[(s_ve, 1)])
                p.emit("vector", lambda e, bk=bk, qt=qt: e.tensor_copy(
                    _hc(qt, 64)[:, :, 32:64], _hc(bk[:, 0:512], 64)[:, :, 32:64]),
                       inc=[(s_ve, 1)])
                ve_rope = s_ve.v
                dst = qfm if sub == 0 else kfm

                def trans(e, qt=qt):
                    ins = None
                    for i in range(4):
                        ins = e.transpose(ptr[:, 128 * i:128 * i + 128],
                                          qt[:, 128 * i:128 * i + 128], ident[:, :])
                    return ins
                p.emit("tensor", trans,
                       wait=[(s_ve, ve_rope), (s_se, sestamp["tr"])], inc=[(s_mm, 1)])
                qk_trans[(t, sub)] = s_mm.v

                def evac(e, t=t, dst=dst):
                    ins = None
                    for i in range(4):
                        ins = e.copy(dst[i][:, 128 * t:128 * t + 128], ptr[:, 128 * i:128 * i + 128])
                    return ins
                p.emit("scalar", evac, wait=[(s_mm, s_mm.v)], inc=[(s_se, 1)])
                sestamp["tr"] = s_se.v
                ve_rope_last = ve_rope
            evst[t] = (max(ve_rope_last, v_ve), sestamp["tr"])
            cb_free[t] = evst[t][0]

        for t in range(NT_ALL + 1):
            if t < NT_ALL:
                p2_s1(t)
            if t >= 1:
                p2_s2(t - 1)
            gamma_pump(p, 1, gbuf)
        last_qkv_mm[g] = s_mm.v
        group_ready_se = s_se.v
        group_ready_ve = s_ve.v

        if g + 1 < min(NG, max(DBG_GROUPS, 1)):
            for dc in range(ND):
                wld = [(s_mm, last_qkv_mm[g])] if dc == 0 else []
                p.emit("gpsimd", lambda e, dc=dc, g=g: e.dma_start(
                    out=wqres[dc][:, :], in_=wqkv[128 * dc:128 * dc + 128, 1536 * (g + 1):1536 * (g + 1) + 1536]),
                       wait=wld, inc=[(s_wk, 16)])
            p.emit("gpsimd", lambda e, g=g: e.dma_start(out=bqg[0:1, :],
                                                        in_=bqkv[0:1, 1536 * (g + 1):1536 * (g + 1) + 1536]),
                   wait=[(s_mm, last_qkv_mm[g])], inc=[(s_wk, 16)])
            wq_loaded[g + 1] = s_wk.v

        if g == NG - 1 and dbg != "p2" and dbg != "p3" and dbg != "p3u" and DBG_GROUPS >= NG:
            for ec in range(ND):
                wld = [(s_mm, last_qkv_mm[g])] if ec == 0 else []
                p.emit("gpsimd", lambda e, ec=ec: e.dma_start(out=wout_sb[ec][:, 0:D],
                                                              in_=wout_in[128 * ec:128 * ec + 128, :]),
                       wait=wld, inc=[(s_wk, 16)])
            wq_loaded["wout"] = s_wk.v

        obm = {0: pb[4], 1: pb[5]}
        avq = {}

        def emit_denoms(hu2):
            for st in (0, 1):
                p.emit("scalar", lambda e, st=st: e.copy(srow[0:1, 0:512], obm[st][64:65, 0:512]),
                       wait=[(s_mm, avq[hu2][st])], inc=[(s_se, 1)])

                def dln(e):
                    with nc.allow_low_precision("softmax denominators via fp16 exp(-ln): row-"
                                                "uniform scale that LN2 cancels"):
                        return e.activation(out=rrow[32:33, 0:512], in_=srow[0:1, 0:512],
                                            func=ACTF.Ln, bias=consts[32:33, C_ZERO:C_ZERO + 1],
                                            scale=consts[32:33, C_ONE:C_ONE + 1])
                p.emit("scalar", dln, wait=[(s_se, s_se.v)], inc=[(s_se, 1)])

                def dexp(e):
                    with nc.allow_low_precision("softmax denominators via fp16 exp(-ln)"):
                        return e.activation(out=rrow[0:1, 0:512], in_=rrow[32:33, 0:512],
                                            func=ACTF.Exp, bias=consts[0:1, C_ZERO:C_ZERO + 1],
                                            scale=consts[0:1, C_NEG1:C_NEG1 + 1])
                p.emit("scalar", dexp, wait=[(s_se, s_se.v), (s_mm, mmstamp.get("bcast", 0))],
                       inc=[(s_se, 1)])
                p.emit("tensor", lambda e: e.matmul(pbc[0:64, 0:512], onesr[0:1, 0:64],
                                                    rrow[0:1, 0:512], start=True, stop=True),
                       wait=[(s_se, s_se.v)], inc=[(s_mm, 1)])
                mmstamp["bcast"] = s_mm.v
                p.emit("scalar", lambda e: e.copy(bcrow[0:64, 0:512], pbc[0:64, 0:512]),
                       wait=[(s_mm, s_mm.v), (s_ve, vstamp.get("norm", 0))], inc=[(s_se, 1)])
                p.emit("vector", lambda e, st=st, hu2=hu2: e.tensor_mul(
                    aout[hu2 // 2][64 * (hu2 % 2):64 * (hu2 % 2) + 64, 512 * st:512 * st + 512],
                    obm[st][0:64, 0:512], bcrow[0:64, 0:512]),
                       wait=[(s_se, s_se.v)], inc=[(s_ve, 1)])
                vstamp["norm"] = s_ve.v
                normst[(hu2, st)] = s_ve.v

        for h in range(GH):
            if g >= DBG_GROUPS or 2 * (GH * g + h) >= DBG_UNITS:
                break
            fc, r0 = h // 2, 64 * (h % 2)
            hu = GH * g + h
            L = [(j, st) for j in range(16) for st in (0, 1) if (j < 4 * st + 4 or j >= 8)]
            ob = {0: pb[4], 1: pb[5]}
            first_st = {st: min(i for i, (jj, ss) in enumerate(L) if ss == st) for st in (0, 1)}
            last_st = {st: max(i for i, (jj, ss) in enumerate(L) if ss == st) for st in (0, 1)}
            exp_st, av_st, msk_st, sc_done = {}, {}, {}, {}
            prev_h_exp = sestamp.get("unit_exp", 0)
            prev_h_av = mmstamp.get("unit_av", 0)
            NCHI = len(L)

            grps = []
            i0 = 0
            while i0 < NCHI:
                if i0 + 1 < NCHI and L[i0][0] == L[i0 + 1][0]:
                    grps.append((i0, i0 + 1))
                    i0 += 2
                else:
                    grps.append((i0,))
                    i0 += 1

            def wmerge(ws):
                d = {}
                for sem, v in ws:
                    if v > d.get(sem, -1):
                        d[sem] = v
                return [(sem, v) for sem, v in d.items()]

            def emit_scores_grp(grp):
                wsc = []
                for ci in grp:
                    if hu == GH * g and ci < 4:
                        wsc += [(s_se, group_ready_se), (s_ve, group_ready_ve)]
                    if ci >= 4:
                        wsc.append((s_se, exp_st[ci - 4]))
                    elif hu > 0:
                        wsc.append((s_se, prev_h_exp))

                def mms(e, grp=grp, fc=fc, r0=r0):
                    ins = None
                    for ci in grp:
                        j, st = L[ci]
                        ins = e.matmul(pb[ci % 4][:, 0:512],
                                       kfm[fc][r0:r0 + 64, 128 * j:128 * j + 128],
                                       qfm[fc][r0:r0 + 64, 512 * st:512 * st + 512],
                                       start=True, stop=True)
                    return ins
                p.emit("tensor", mms, wait=wmerge(wsc), inc=[(s_mm, 1)])
                for ci in grp:
                    sc_done[ci] = s_mm.v
                for ci in grp:
                    j, st = L[ci]
                    at = attnT[ci % 3]
                    wex = [(s_mm, sc_done[ci])]
                    if ci >= 3:
                        wex.append((s_mm, av_st[ci - 3]))
                    elif hu > 0:
                        wex.append((s_mm, prev_h_av))
                    p.emit("scalar", lambda e, sbk=pb[ci % 4], at=at, st=st, j=j: e.activation(
                        out=at[:, 0:512], in_=sbk[:, 0:512], func=ACTF.Exp,
                        bias=btbl[:, 16 * st + j:16 * st + j + 1],
                        scale=consts[:, C_EXPSC:C_EXPSC + 1]),
                           wait=wex, inc=[(s_se, 1)])
                    exp_st[ci] = s_se.v
                    if 4 * st <= j < 4 * st + 4:
                        m = j - 4 * st
                        p.emit("vector", lambda e, at=at, m=m: e.tensor_mul(
                            at[:, 0:512], at[:, 0:512], mband[:, 384 - 128 * m:896 - 128 * m]),
                               wait=[(s_se, exp_st[ci])], inc=[(s_ve, 1)])
                        msk_st[ci] = s_ve.v

            def emit_av_grp(grp):
                wav = []
                for ci in grp:
                    j, st = L[ci]
                    wav.append((s_ve, msk_st[ci]) if ci in msk_st else (s_se, exp_st[ci]))
                    if ci == first_st[st]:
                        if hu >= 1:
                            wav.append((s_ve, normst[(hu - 1, st)]))
                        if hu == GH * g or hu == 0:
                            wav.append((s_ve, group_ready_ve))

                def mms(e, grp=grp, h=h):
                    ins = None
                    for ci in grp:
                        j, st = L[ci]
                        ins = e.matmul(ob[st][0:65, 0:512], vtm[j][:, 65 * h:65 * h + 65],
                                       attnT[ci % 3][:, 0:512],
                                       start=(ci == first_st[st]), stop=(ci == last_st[st]))
                    return ins
                p.emit("tensor", mms, wait=wmerge(wav), inc=[(s_mm, 1)])
                for ci in grp:
                    av_st[ci] = s_mm.v

            avp = {"next": 0, "max": -1, "denoms": hu > GH * g}

            def pump_av_upto(c_needed):
                while avp["max"] < c_needed and avp["next"] < len(grps):
                    if avp["denoms"]:
                        emit_denoms(hu - 1)
                        avp["denoms"] = False
                    gi2 = avp["next"]
                    emit_av_grp(grps[gi2])
                    avp["next"] = gi2 + 1
                    avp["max"] = max(grps[gi2])

            pumped = set()
            for grp in grps:
                pump_av_upto(max(grp) - 3)
                emit_scores_grp(grp)
                for mark in (10, 20):
                    if max(grp) >= mark and mark not in pumped:
                        pumped.add(mark)
                        gamma_pump(p, 3, gbuf)
            pump_av_upto(NCHI - 1)
            sestamp["unit_exp"] = exp_st[NCHI - 1]
            mmstamp["unit_av"] = av_st[NCHI - 1]
            avq[hu] = {st: av_st[last_st[st]] for st in (0, 1)}

        if avq:
            emit_denoms(max(avq))
        if dbg == "fd" and g == 0:
            p.emit("sync", lambda e: e.dma_start(out=dbg_outs["kf0"][:, :], in_=kfm[0][:, :]),
                   wait=[(s_se, group_ready_se)], inc=[(s_dbg, 16)])
            p.emit("sync", lambda e: e.dma_start(out=dbg_outs["qf0"][:, :], in_=qfm[0][:, :]),
                   inc=[(s_dbg, 16)])
            p.emit("sync", lambda e: e.dma_start(out=dbg_outs["vt0"][:, :], in_=vtm[0][:, :]),
                   inc=[(s_dbg, 16)])
            p.emit("sync", lambda e: e.dma_start(out=dbg_outs["l10"][:, :], in_=ln1[0][:, :]),
                   inc=[(s_dbg, 16)])
            p.emit("sync", lambda e: e.dma_start(out=dbg_outs["l1t"][:, :], in_=ln1tm[0][:, :])
                   if False else e.nop(), inc=[])
        if g == min(NG, max(DBG_GROUPS, 1)) - 1 and dbg != "thr":
            gamma_pump(p, NGC, gbuf)
            gamma_finish(p)
        p.pour(nc, f"p23_{g}")

    nc._dbg = dict(qk_tm=qk_tm, rsa=rsa, rsb=rsb, cbuf=cbuf, qfm=qfm, kfm=kfm,
                   vtm=vtm, ln1=ln1, aout=aout)
    if dbg == "p3u":
        p = Plan()
        for i in range(2):
            p.emit("sync", lambda e, i=i: e.dma_start(out=dbg_outs["at"][128 * i:128 * i + 128, :],
                                                      in_=attnT[i][:, :]), inc=[(s_dbg, 16)])
        p.emit("sync", lambda e: e.dma_start(out=dbg_outs["srow"][:, :], in_=srow[:, :]), inc=[(s_dbg, 16)])
        p.emit("sync", lambda e: e.dma_start(out=dbg_outs["bc"][:, :], in_=bcrow[:, :]), inc=[(s_dbg, 16)])
        p.emit("sync", lambda e: e.dma_start(out=dbg_outs["ao0"][:, :], in_=aout[0][:, :]), inc=[(s_dbg, 16)])
        p.emit("sync", lambda e: e.nop(), wait=[(s_dbg, s_dbg.v), (s_thr, 16)])
        p.pour(nc, "dbgf")
        return nc

    if dbg == "p2":
        p = Plan()
        for i in range(4):
            p.emit("sync", lambda e, i=i: e.dma_start(out=dbg_outs["qfm"][128 * i:128 * i + 128, :],
                                                      in_=qfm[i][:, :]), inc=[(s_dbg, 16)])
            p.emit("sync", lambda e, i=i: e.dma_start(out=dbg_outs["kfm"][128 * i:128 * i + 128, :],
                                                      in_=kfm[i][:, :]), inc=[(s_dbg, 16)])
        for t in range(NT_ALL):
            p.emit("sync", lambda e, t=t: e.dma_start(out=dbg_outs["vtm"][128 * t:128 * t + 128, :],
                                                      in_=vtm[t][:, :]), inc=[(s_dbg, 16)])
        p.emit("sync", lambda e: e.nop(), wait=[(s_dbg, s_dbg.v), (s_thr, 16)])
        p.pour(nc, "dbgf")
        return nc

    if dbg == "p3":
        p = Plan()
        for i in range(ND):
            p.emit("sync", lambda e, i=i: e.dma_start(out=dbg_outs["aout"][128 * i:128 * i + 128, :],
                                                      in_=aout[i][:, :]), inc=[(s_dbg, 16)])
        p.emit("sync", lambda e: e.nop(), wait=[(s_dbg, s_dbg.v), (s_thr, 16)])
        p.pour(nc, "dbgf")
        return nc

    stk23.close()
    stkg.close()

    stk5o = ExitStack()
    E5o = lambda name, shape, dt: stk5o.enter_context(nc.sbuf_tensor(name, shape, dt))
    x2 = [E5o(f"x2_{t}", [128, D], F16) for t in range(NT_OWN)]
    scr5 = E5o("scr5", [128, 2048], F32)
    bout_sb = E5o("bout_sb", [1, D], F16)

    stk5 = ExitStack()
    E5 = lambda name, shape, dt: stk5.enter_context(nc.sbuf_tensor(name, shape, dt))
    xb5 = [E5(f"xb5_{i}", [128, D], F32) for i in range(3)]
    y1 = [E5(f"y1_{i}", [128, 2048], F32) for i in range(2)]
    l3t = [E5(f"l3t{i}", [128, D], F16) for i in range(2)]
    scr5b = E5("scr5b", [128, 2048], F32)
    scr5p = [scr5, scr5b]

    p = Plan()
    p.emit("vector", lambda e: e.memset(scr5[:, :], 0.0), inc=[(s_ve, 1)])
    p.emit("vector", lambda e: e.memset(scr5b[:, :], 0.0), inc=[(s_ve, 1)])
    p.emit("vector", lambda e: e.memset(y1[0][:, :], 0.0), inc=[(s_ve, 1)])
    p.emit("vector", lambda e: e.memset(y1[1][:, :], 0.0), inc=[(s_ve, 1)])
    if dbg == "fd":
        for i in range(ND):
            p.emit("sync", lambda e, i=i: e.dma_start(out=dbg_outs["ao"][128 * i:128 * i + 128, :],
                                                      in_=aout[i][:, :]), inc=[(s_dbg, 16)])
        AO_DUMP = s_dbg.v
    p.emit("gpsimd", lambda e: e.dma_start(out=bout_sb[:, :], in_=bout[:, :]), inc=[(s_wk, 16)])
    if "wout" not in wq_loaded:
        for ec in range(ND):
            p.emit("gpsimd", lambda e, ec=ec: e.dma_start(out=wout_sb[ec][:, 0:D],
                                                          in_=wout_in[128 * ec:128 * ec + 128, :]),
                   inc=[(s_wk, 16)])
    wout_wk = s_wk.v

    p5_evst = {}
    x2add = {}
    opst = {}
    l3st = {}

    def p5_mm(t):
        sx5 = s_xp[t % 2]
        p.emit("sync", lambda e, t=t: e.dma_start(out=xb5[t % 3][:, :], in_=xin[128 * t:128 * t + 128, :]),
               wait=([(s_ve, x2add[t - 3])] if t >= 3 else []), inc=[(sx5, 16)])
        x2add[("sem", t)] = (sx5, sx5.v)
        wmm = [(s_wk, wout_wk)]
        if t >= 1:
            wmm.append((s_ve, p5_evst[t - 1]))

        def op_mms(e, t=t):
            ins = None
            for dg in range(4):
                for ec in range(ND):
                    ins = e.matmul(pb[dg][:, 0:512], aout[ec][:, 128 * t:128 * t + 128],
                                   wout_sb[ec][:, 512 * dg:512 * dg + 512],
                                   start=(ec == 0), stop=False)
                ins = e.matmul(pb[dg][:, 0:512], onesr[0:1, 0:128],
                               bout_sb[0:1, 512 * dg:512 * dg + 512], start=False, stop=True)
            return ins
        p.emit("tensor", op_mms, wait=wmm, inc=[(s_mm, 1)])
        opst[t] = s_mm.v

    def p5_evac(t):
        def op_evac(e, t=t):
            ins = None
            for dg in range(4):
                ins = e.tensor_copy(y1[t % 2][:, 512 * dg:512 * dg + 512], pb[dg][:, 0:512])
            return ins
        p.emit("vector", op_evac, wait=[(s_mm, opst[t])], inc=[(s_ve, 1)])
        p5_evst[t] = s_ve.v

    def p5_ladder(t):
        sb = 16 * (t % 2)
        yy = y1[t % 2]
        sc = scr5p[t % 2]
        x_sem, x_stamp = x2add[("sem", t)]
        p.emit("vector", lambda e, yy=yy, sb=sb: e.reduce_sum(stat[:, sb:sb + 1], yy[:, :], axis=AX),
               wait=[(s_ve, p5_evst[t])])
        p.emit("scalar", lambda e, yy=yy, sc=sc: e.square(sc[:, :], yy[:, :]),
               wait=[(s_ve, p5_evst[t])], inc=[(s_se, 1)])
        p.emit("vector", lambda e, sb=sb, sc=sc: e.reduce_sum(stat[:, sb + 1:sb + 2], sc[:, :], axis=AX),
               wait=[(s_se, s_se.v)], inc=[(s_ve, 1)])
        p.emit("vector", lambda e, sb=sb: e.tensor_scalar_mul(stat[:, sb + 2:sb + 3], stat[:, sb:sb + 1], 1.0 / D),
               wait=[(s_ve, s_ve.v)], inc=[(s_ve, 1)])
        p.emit("vector", lambda e, sb=sb: e.tensor_scalar_mul(stat[:, sb + 3:sb + 4], stat[:, sb + 1:sb + 2], 1.0 / D),
               wait=[(s_ve, s_ve.v)], inc=[(s_ve, 1)])
        p.emit("vector", lambda e, sb=sb: e.scalar_tensor_tensor(out=stat[:, sb + 4:sb + 5], in0=stat[:, sb + 2:sb + 3],
                                                                 scalar=1.0, in1=stat[:, sb + 2:sb + 3],
                                                                 op0=ALU.mult, op1=ALU.mult),
               wait=[(s_ve, s_ve.v)], inc=[(s_ve, 1)])
        p.emit("vector", lambda e, sb=sb: e.tensor_sub(stat[:, sb + 5:sb + 6], stat[:, sb + 3:sb + 4],
                                                       stat[:, sb + 4:sb + 5]),
               wait=[(s_ve, s_ve.v)], inc=[(s_ve, 1)])
        p.emit("scalar", lambda e, sb=sb: e.activation(out=stat[:, sb + 7:sb + 8], in_=stat[:, sb + 5:sb + 6],
                                                       func=ACTF.Ln, bias=consts[:, C_EPS:C_EPS + 1],
                                                       scale=consts[:, C_ONE:C_ONE + 1]),
               wait=[(s_ve, s_ve.v)], inc=[(s_se, 1)])
        p.emit("scalar", lambda e, sb=sb: e.activation(out=stat[:, sb + 6:sb + 7], in_=stat[:, sb + 7:sb + 8],
                                                       func=ACTF.Exp, bias=consts[:, C_ZERO:C_ZERO + 1],
                                                       scale=consts[:, C_NHALF:C_NHALF + 1]),
               wait=[(s_se, s_se.v)], inc=[(s_se, 1)])
        p.emit("vector", lambda e, yy=yy, sb=sb, sc=sc: e.tensor_scalar(
            out=sc[:, :], in0=yy[:, :], scalar1=stat[:, sb + 2:sb + 3], scalar2=stat[:, sb + 6:sb + 7],
            op0=ALU.subtract, op1=ALU.mult),
               wait=[(s_se, s_se.v)], inc=[(s_ve, 1)])
        p.emit("vector", lambda e, t=t, sc=sc: e.tensor_add(x2[t][:, :], sc[:, :], xb5[t % 3][:, :]),
               wait=[(x_sem, x_stamp)], inc=[(s_ve, 1)])
        x2add[t] = s_ve.v
        p.emit("vector", lambda e, t=t, sb=sb: e.reduce_sum(stat[:, sb + 8:sb + 9], x2[t][:, :], axis=AX),
               wait=[(s_ve, s_ve.v)])
        p.emit("scalar", lambda e, t=t, sc=sc: e.square(sc[:, :], x2[t][:, :]),
               wait=[(s_ve, x2add[t])], inc=[(s_se, 1)])
        p.emit("vector", lambda e, sb=sb, sc=sc: e.reduce_sum(stat[:, sb + 9:sb + 10], sc[:, :], axis=AX),
               wait=[(s_se, s_se.v)], inc=[(s_ve, 1)])
        p.emit("vector", lambda e, sb=sb: e.tensor_scalar_mul(stat[:, sb + 10:sb + 11], stat[:, sb + 8:sb + 9], 1.0 / D),
               wait=[(s_ve, s_ve.v)], inc=[(s_ve, 1)])
        p.emit("vector", lambda e, sb=sb: e.tensor_scalar_mul(stat[:, sb + 11:sb + 12], stat[:, sb + 9:sb + 10], 1.0 / D),
               wait=[(s_ve, s_ve.v)], inc=[(s_ve, 1)])
        p.emit("vector", lambda e, sb=sb: e.scalar_tensor_tensor(out=stat[:, sb + 12:sb + 13],
                                                                 in0=stat[:, sb + 10:sb + 11], scalar=1.0,
                                                                 in1=stat[:, sb + 10:sb + 11],
                                                                 op0=ALU.mult, op1=ALU.mult),
               wait=[(s_ve, s_ve.v)], inc=[(s_ve, 1)])
        p.emit("vector", lambda e, sb=sb: e.tensor_sub(stat[:, sb + 13:sb + 14], stat[:, sb + 11:sb + 12],
                                                       stat[:, sb + 12:sb + 13]),
               wait=[(s_ve, s_ve.v)], inc=[(s_ve, 1)])
        p.emit("scalar", lambda e, sb=sb: e.activation(out=stat[:, sb + 15:sb + 16], in_=stat[:, sb + 13:sb + 14],
                                                       func=ACTF.Ln, bias=consts[:, C_EPS:C_EPS + 1],
                                                       scale=consts[:, C_ONE:C_ONE + 1]),
               wait=[(s_ve, s_ve.v)], inc=[(s_se, 1)])
        p.emit("scalar", lambda e, sb=sb: e.activation(out=stat[:, sb + 14:sb + 15], in_=stat[:, sb + 15:sb + 16],
                                                       func=ACTF.Exp, bias=consts[:, C_ZERO:C_ZERO + 1],
                                                       scale=consts[:, C_NHALF:C_NHALF + 1]),
               wait=[(s_se, s_se.v)], inc=[(s_se, 1)])
        wl3 = [(s_se, s_se.v)]
        if t >= 2:
            wl3.append((s_mm, l3st[t - 2]))
        p.emit("vector", lambda e, t=t, sb=sb: e.tensor_scalar(out=l3t[t % 2][:, :], in0=x2[t][:, :],
                                                               scalar1=stat[:, sb + 10:sb + 11],
                                                               scalar2=stat[:, sb + 14:sb + 15],
                                                               op0=ALU.subtract, op1=ALU.mult),
               wait=wl3, inc=[(s_ve, 1)])
        l3_ve = s_ve.v
        for fq in range(4):
            def trans(e, t=t, fq=fq):
                ins = None
                for i in range(4):
                    dc = 4 * fq + i
                    ins = e.transpose(ptr[:, 128 * i:128 * i + 128],
                                      l3t[t % 2][:, 128 * dc:128 * dc + 128], ident[:, :])
                return ins
            wev = [(s_ve, l3_ve), (s_se, sestamp["tr"])]
            if dbg == "fd" and t == 0 and fq == 0:
                wev.append((s_dbg, AO_DUMP))
            p.emit("tensor", trans, wait=wev, inc=[(s_mm, 1)])

            def evac(e, t=t, fq=fq):
                ins = None
                for i in range(4):
                    dc = 4 * fq + i
                    ins = e.copy(ln3_fm[dc][:, 128 * t:128 * t + 128], ptr[:, 128 * i:128 * i + 128])
                return ins
            p.emit("scalar", evac, wait=[(s_mm, s_mm.v)], inc=[(s_se, 1)])
            sestamp["tr"] = s_se.v
        l3st[t] = s_mm.v
        if dbg == "p5a" and t == 0:
            pass

    for t in range(NT_OWN + 2):
        if 1 <= t <= NT_OWN:
            p5_evac(t - 1)
        if t < NT_OWN:
            p5_mm(t)
        if t >= 2:
            p5_ladder(t - 2)
    ln3_se = s_se.v
    p.pour(nc, "p5")
    if dbg == "p5a":
        p = Plan()
        p.emit("sync", lambda e: e.nop(), wait=[(s_dbg, s_dbg.v), (s_thr, 16)])
        p.pour(nc, "dbgf")
        return nc

    if dbg == "p5":
        p = Plan()
        for t in range(NT_OWN):
            p.emit("sync", lambda e, t=t: e.dma_start(out=dbg_outs["x2"][128 * t:128 * t + 128, :],
                                                      in_=x2[t][:, :]), inc=[(s_dbg, 16)])
        for dc in range(ND):
            p.emit("sync", lambda e, dc=dc: e.dma_start(out=dbg_outs["ln3"][128 * dc:128 * dc + 128, :],
                                                        in_=ln3_fm[dc][:, :]), inc=[(s_dbg, 16)])
        p.emit("sync", lambda e: e.nop(), wait=[(s_dbg, s_dbg.v), (s_thr, 16)])
        p.pour(nc, "dbgf")
        return nc

    stk5.close()

    stk6 = ExitStack()
    E6 = lambda name, shape, dt: stk6.enter_context(nc.sbuf_tensor(name, shape, dt))
    w3t = [E6(f"w3t{i}", [128, 2048], F16) for i in range(2 * YBLK)]
    hid = [E6(f"hid{i}", [128, TOK_OWN], F16) for i in range(8)]
    sgt = E6("sgt", [128, TOK_OWN], F16)
    tscr = E6("tscr", [128, 512], F16)

    yt = ln1

    def w12tile(wgt, par, dc):
        base = 8 + 4 * wgt + 2 * par + dc // 8
        c0 = 256 * (dc % 8)
        return ln1[base][:, c0:c0 + 256]

    p = Plan()
    NSB = NHC // 2
    NBLK = NHC // YBLK
    GB_ALL = [(sb_, sb_.v) for sb_ in s_gbp]
    w12_fw, w3_fw = {}, {}
    tern12_ve, tern3_ve = {}, {}
    mm12_done, mm3_done = {}, {}
    sig_done, hid_done = {}, {}
    mm3_evac = {}

    def stream_w12(sbi):
        par = sbi % 2
        sfw = s_fw12[par]
        first_wait = list(GB_ALL) + [(s_thr, 16)]
        if sbi >= 2:
            first_wait.append((s_mm, mm12_done[sbi - 2]))
        first = True
        for wgt, src in ((0, w1f), (1, w2f)):
            for dc in range(ND):
                p.emit("sync", lambda e, src=src, dc=dc, sbi=sbi, wgt=wgt, par=par: e.dma_start(
                    out=w12tile(wgt, par, dc),
                    in_=src[256 * sbi:256 * sbi + 256, 128 * dc:128 * dc + 128], transpose=True),
                       wait=(first_wait if first else []), inc=[(sfw, 16)])
                first = False
        w12_fw[sbi] = (sfw, sfw.v)

        def tern(e, par=par):
            ins = None
            for wgt in range(2):
                for half in range(8):
                    base = 8 + 4 * wgt + 2 * par + half // 4
                    c0 = 512 * (half % 4)
                    tgt = ln1[base][:, c0:c0 + 512]
                    ins = e.tensor_scalar(out=tscr[:, 0:512], in0=tgt, scalar1=thr[:, 4 + wgt:5 + wgt],
                                          scalar2=1.0, op0=ALU.is_gt, op1=ALU.subtract)
                    ins = e.scalar_tensor_tensor(out=tgt, in0=tgt, scalar=thr[:, wgt:wgt + 1],
                                                 in1=tscr[:, 0:512], op0=ALU.is_ge, op1=ALU.add)
            return ins
        p.emit("vector", tern, wait=[w12_fw[sbi], (s_thr, 16)], inc=[(s_ve, 1)])
        tern12_ve[sbi] = s_ve.v

    def stream_w3(bi):
        par = bi % 2
        sfw = s_fw3[par]
        first_wait = list(GB_ALL) + [(s_thr, 16)]
        if bi >= 2:
            first_wait.append((s_ve, mm3_evac.get("last", 0)))
            first_wait.append((s_mm, mm3_done[bi - 2]))
        first = True
        for i in range(YBLK):
            p.emit("sync", lambda e, i=i, bi=bi, par=par: e.dma_start(
                out=w3t[YBLK * par + i][:, :],
                in_=w3f[0:D, 128 * (YBLK * bi + i):128 * (YBLK * bi + i) + 128], transpose=True),
                   wait=(first_wait if first else []), inc=[(sfw, 16)])
            first = False
        w3_fw[bi] = (sfw, sfw.v)

        def tern(e, par=par):
            ins = None
            for i in range(YBLK):
                for c0 in range(0, 2048, 512):
                    tgt = w3t[YBLK * par + i][:, c0:c0 + 512]
                    ins = e.tensor_scalar(out=tscr[:, 0:512], in0=tgt, scalar1=thr[:, 6:7],
                                          scalar2=1.0, op0=ALU.is_gt, op1=ALU.subtract)
                    ins = e.scalar_tensor_tensor(out=tgt, in0=tgt, scalar=thr[:, 2:3],
                                                 in1=tscr[:, 0:512], op0=ALU.is_ge, op1=ALU.add)
            return ins
        p.emit("vector", tern, wait=[w3_fw[bi], (s_thr, 16)], inc=[(s_ve, 1)])
        tern3_ve[bi] = s_ve.v

    stream_w12(0)
    stream_w12(1)
    stream_w3(0)
    for bi in range(NBLK):
        par = bi % 2
        for i in range(YBLK):
            hc = YBLK * bi + i
            sbi = hc // 2
            spar = sbi % 2
            for tg in range(2):
                wml = [(s_ve, tern12_ve[sbi])]
                if bi == 0 and i == 0 and tg == 0:
                    wml.append((s_se, ln3_se))
                if hc >= 1:
                    wml.append((s_se, sig_done.get((hc - 1, tg), 0)))
                    wml.append((s_ve, hid_done.get((hc - 1, tg), 0)))

                def mm12(e, hc=hc, tg=tg, spar=spar):
                    ins = None
                    for wgt in range(2):
                        bank = pb[2 * wgt + tg]
                        for dc in range(ND):
                            lt = w12tile(wgt, spar, dc)[:, 128 * (hc % 2):128 * (hc % 2) + 128]
                            ins = e.matmul(bank[:, 0:512], lt,
                                           ln3_fm[dc][:, 512 * tg:512 * tg + 512],
                                           start=(dc == 0), stop=(dc == ND - 1))
                    return ins
                p.emit("tensor", mm12, wait=wml, inc=[(s_mm, 1)])
                mm12_done[sbi] = s_mm.v
                wsg = [(s_mm, s_mm.v)]
                if hc >= 1:
                    wsg.append((s_ve, hid_done.get((hc - 1, tg), 0)))
                p.emit("scalar", lambda e, tg=tg: e.activation(
                    out=sgt[:, 512 * tg:512 * tg + 512], in_=pb[tg][:, 0:512], func=ACTF.Sigmoid),
                       wait=wsg, inc=[(s_se, 1)])
                sig_done[(hc, tg)] = s_se.v
                whid = [(s_se, s_se.v)]
                if hc >= 8:
                    whid.append((s_mm, mm3_done[bi - 2]))
                p.emit("vector", lambda e, hc=hc, tg=tg: e.tensor_mul(
                    hid[hc % 8][:, 512 * tg:512 * tg + 512],
                    pb[2 + tg][:, 0:512], sgt[:, 512 * tg:512 * tg + 512]),
                       wait=whid, inc=[(s_ve, 1)])
                hid_done[(hc, tg)] = s_ve.v
        hid_all = s_ve.v
        for t in range(NT_OWN):
            for dg in range(4):
                bank = pb[4 + ((t * 4 + dg) % 2)]

                def mm3(e, t=t, dg=dg, bi=bi, par=par, bank=bank):
                    ins = None
                    for i in range(YBLK):
                        ins = e.matmul(bank[:, 0:512],
                                       hid[(YBLK * bi + i) % 8][:, 128 * t:128 * t + 128],
                                       w3t[YBLK * par + i][:, 512 * dg:512 * dg + 512],
                                       start=(i == 0), stop=(i == YBLK - 1))
                    return ins
                wm3 = [(s_ve, hid_all), (s_ve, tern3_ve[bi])]
                prev = mm3_evac.get((t * 4 + dg) % 2)
                if prev:
                    wm3.append((s_ve, prev))
                p.emit("tensor", mm3, wait=wm3, inc=[(s_mm, 1)])
                mm3_done[bi] = s_mm.v

                if bi == 0:
                    p.emit("vector", lambda e, t=t, dg=dg, bank=bank: e.tensor_copy(
                        yt[t][:, 512 * dg:512 * dg + 512], bank[:, 0:512]),
                           wait=[(s_mm, s_mm.v)], inc=[(s_ve, 1)])
                else:
                    p.emit("vector", lambda e, t=t, dg=dg, bank=bank: e.tensor_add(
                        yt[t][:, 512 * dg:512 * dg + 512], yt[t][:, 512 * dg:512 * dg + 512],
                        bank[:, 0:512]),
                           wait=[(s_mm, s_mm.v)], inc=[(s_ve, 1)])
                mm3_evac[(t * 4 + dg) % 2] = s_ve.v
                mm3_evac["last"] = s_ve.v
        y_ve = s_ve.v
        if 2 * bi + 2 < NSB:
            stream_w12(2 * bi + 2)
        if 2 * bi + 3 < NSB:
            stream_w12(2 * bi + 3)
        if bi + 1 < NBLK:
            stream_w3(bi + 1)
    p.pour(nc, "p6")

    if dbg == "p6":
        p = Plan()
        for t in range(NT_OWN):
            p.emit("sync", lambda e, t=t: e.dma_start(out=dbg_outs["y"][128 * t:128 * t + 128, :],
                                                      in_=yt[t][:, 0:D]),
                   wait=([(s_ve, y_ve)] if t == 0 else []), inc=[(s_dbg, 16)])
        p.emit("sync", lambda e: e.nop(), wait=[(s_dbg, s_dbg.v)])
        p.pour(nc, "dbgf")
        return nc

    stk6.close()
    stk7 = ExitStack()
    E7 = lambda name, shape, dt: stk7.enter_context(nc.sbuf_tensor(name, shape, dt))
    obuf = [E7(f"obuf{i}", [128, D], F32) for i in range(2)]

    p = Plan()
    if dbg == "fd":
        for t in range(NT_OWN):
            p.emit("sync", lambda e, t=t: e.dma_start(out=dbg_outs["x2d"][128 * t:128 * t + 128, :],
                                                      in_=x2[t][:, :]), inc=[(s_dbg, 16)])
            p.emit("sync", lambda e, t=t: e.dma_start(out=dbg_outs["yd"][128 * t:128 * t + 128, :],
                                                      in_=yt[t][:, 0:D]),
                   wait=([(s_ve, y_ve)] if t == 0 else []), inc=[(s_dbg, 16)])
    add7 = {}
    r7 = {}

    def p7_s1(t):
        sb = 16 * (t % 2)
        sc = scr5p[t % 2]
        p.emit("vector", lambda e, t=t, sb=sb: e.reduce_sum(stat[:, sb:sb + 1], yt[t][:, 0:D], axis=AX),
               wait=[(s_ve, y_ve)])
        wsq7 = [(s_ve, y_ve)]
        if t >= 2:
            wsq7.append((s_ve, add7[t - 2]))
        p.emit("scalar", lambda e, t=t, sc=sc: e.square(sc[:, :], yt[t][:, 0:D]),
               wait=wsq7, inc=[(s_se, 1)])
        p.emit("vector", lambda e, sb=sb, sc=sc: e.reduce_sum(stat[:, sb + 1:sb + 2], sc[:, :], axis=AX),
               wait=[(s_se, s_se.v)], inc=[(s_ve, 1)])
        p.emit("vector", lambda e, sb=sb: e.tensor_scalar_mul(stat[:, sb + 2:sb + 3], stat[:, sb:sb + 1], 1.0 / D),
               wait=[(s_ve, s_ve.v)], inc=[(s_ve, 1)])
        p.emit("vector", lambda e, sb=sb: e.tensor_scalar_mul(stat[:, sb + 3:sb + 4], stat[:, sb + 1:sb + 2], 1.0 / D),
               wait=[(s_ve, s_ve.v)], inc=[(s_ve, 1)])
        p.emit("vector", lambda e, sb=sb: e.scalar_tensor_tensor(out=stat[:, sb + 4:sb + 5], in0=stat[:, sb + 2:sb + 3],
                                                                 scalar=1.0, in1=stat[:, sb + 2:sb + 3],
                                                                 op0=ALU.mult, op1=ALU.mult),
               wait=[(s_ve, s_ve.v)], inc=[(s_ve, 1)])
        p.emit("vector", lambda e, sb=sb: e.tensor_sub(stat[:, sb + 5:sb + 6], stat[:, sb + 3:sb + 4],
                                                       stat[:, sb + 4:sb + 5]),
               wait=[(s_ve, s_ve.v)], inc=[(s_ve, 1)])
        p.emit("scalar", lambda e, sb=sb: e.activation(out=stat[:, sb + 7:sb + 8], in_=stat[:, sb + 5:sb + 6],
                                                       func=ACTF.Ln, bias=consts[:, C_EPS:C_EPS + 1],
                                                       scale=consts[:, C_ONE:C_ONE + 1]),
               wait=[(s_ve, s_ve.v)], inc=[(s_se, 1)])
        p.emit("scalar", lambda e, sb=sb: e.activation(out=stat[:, sb + 6:sb + 7], in_=stat[:, sb + 7:sb + 8],
                                                       func=ACTF.Exp, bias=consts[:, C_ZERO:C_ZERO + 1],
                                                       scale=consts[:, C_NHALF:C_NHALF + 1]),
               wait=[(s_se, s_se.v)], inc=[(s_se, 1)])
        r7[t] = s_se.v

    def p7_s2(t):
        sb = 16 * (t % 2)
        sc = scr5p[t % 2]
        sst = s_stp[t % 2]
        p.emit("vector", lambda e, t=t, sb=sb, sc=sc: e.tensor_scalar(
            out=sc[:, :], in0=yt[t][:, 0:D], scalar1=stat[:, sb + 2:sb + 3],
            scalar2=stat[:, sb + 6:sb + 7], op0=ALU.subtract, op1=ALU.mult),
               wait=[(s_se, r7[t])], inc=[(s_ve, 1)])
        wadd = [(s_ve, s_ve.v)]
        if t >= 2:
            wadd.append((sst, sst.v))
        p.emit("vector", lambda e, t=t, sc=sc: e.tensor_add(obuf[t % 2][:, :], sc[:, :], x2[t][:, :]),
               wait=wadd, inc=[(s_ve, 1)])
        add7[t] = s_ve.v
        p.emit("sync", lambda e, t=t: e.dma_start(out=yout[128 * t:128 * t + 128, :], in_=obuf[t % 2][:, :]),
               wait=[(s_ve, s_ve.v)], inc=[(sst, 16)])

    for t in range(NT_OWN + 1):
        if t < NT_OWN:
            p7_s1(t)
        if t >= 1:
            p7_s2(t - 1)
    p.emit("sync", lambda e: e.nop(), wait=[(s_stp[0], s_stp[0].v), (s_stp[1], s_stp[1].v)])
    p.pour(nc, "p7")
    stk7.close()
    stk5o.close()
    pstk.close()
    return nc



def _host_inputs(L0_attn_x, L0_attn_Wqkv, L0_attn_bqkv, L0_attn_Wout, L0_attn_bout,
                 L0_ffn_w1, L0_ffn_w2, L0_ffn_w3):
    x = np.asarray(L0_attn_x, dtype=np.float32)
    Wqkv = np.ascontiguousarray(np.asarray(L0_attn_Wqkv, dtype=np.float32))
    bq = np.asarray(L0_attn_bqkv, dtype=np.float32)
    Wout = np.ascontiguousarray(np.asarray(L0_attn_Wout, dtype=np.float32))
    bo = np.asarray(L0_attn_bout, dtype=np.float32)
    w1 = np.ascontiguousarray(np.asarray(L0_ffn_w1, dtype=np.float32))
    w2 = np.ascontiguousarray(np.asarray(L0_ffn_w2, dtype=np.float32))
    w3 = np.ascontiguousarray(np.asarray(L0_ffn_w3, dtype=np.float32))

    perm = []
    for g in range(NG):
        for part in range(3):
            base = part * D + 512 * g
            perm += list(range(base, base + 512))
    wqkv_r = np.ascontiguousarray(Wqkv[:, perm])
    bqkv_r = np.ascontiguousarray(bq[perm][None, :])

    inv = (1.0 / 10000.0 ** (np.arange(0, ROT, 2, dtype=np.float32) / ROT)).astype(np.float32)

    tri = (np.arange(128)[:, None] <= np.arange(128)[None, :]).astype(np.float32)
    mb = np.zeros((128, 896), dtype=np.float32)
    mb[:, 384:512] = tri
    mb[:, 512:896] = 1.0
    mband = mb.astype(np.float16)

    ident = np.eye(128, dtype=np.float16)

    consts = np.zeros((128, NCONST), dtype=np.float32)
    consts[:, C_EXPSC] = 0.125
    consts[:, C_EPS] = 1e-5
    consts[:, C_NHALF] = -0.5
    consts[:, C_INVD] = 1.0 / D
    consts[:, C_THRSC] = 0.5 / (HFF * D)
    consts[:, C_THRB] = 0.5e-8
    consts[:, C_ONE] = 1.0
    consts[:, C_NEG1] = -1.0
    consts[:, C_ZERO] = 0.0

    in_maps = []
    for c in range(NCORE):
        b, half = c // 2, c % 2
        own = slice(half * TOK_OWN, (half + 1) * TOK_OWN)
        oth = slice((1 - half) * TOK_OWN, (2 - half) * TOK_OWN)
        xr = np.ascontiguousarray(np.concatenate([x[b, own], x[b, oth]], axis=0))
        pos = np.concatenate([np.arange(half * TOK_OWN, (half + 1) * TOK_OWN),
                              np.arange((1 - half) * TOK_OWN, (2 - half) * TOK_OWN)]).astype(np.float32)
        fr = np.outer(pos, inv)
        cosv, sinv = np.cos(fr).astype(np.float32), np.sin(fr).astype(np.float32)
        c1h = np.concatenate([cosv, cosv], axis=1)
        c2h = np.concatenate([-sinv, sinv], axis=1)
        c1 = np.ascontiguousarray(np.tile(c1h, (1, GH)).astype(np.float16))
        c2 = np.ascontiguousarray(np.tile(c2h, (1, GH)).astype(np.float16))

        btblv = np.full((128, 32), -2.0, dtype=np.float32)
        if half == 0:
            for st in range(2):
                for j in range(8, 16):
                    btblv[:, st * 16 + j] = -1250.0

        in_maps.append({
            "xin": xr, "wqkv": wqkv_r, "bqkv": bqkv_r, "wout": Wout, "bout": bo[None, :],
            "w1": w1, "w2": w2, "w3": w3, "c1": c1, "c2": c2, "btbl": btblv,
            "mband": mband, "ident": ident, "consts": consts,
        })
    return in_maps


def kernel(**inputs):
    in_maps = _host_inputs(**inputs)
    nc = build()
    res = run_bass_kernel_spmd(nc, in_maps, core_ids=list(range(NCORE)))
    out = np.zeros((B, S, D), dtype=np.float32)
    for c in range(NCORE):
        b, half = c // 2, c % 2
        out[b, half * TOK_OWN:(half + 1) * TOK_OWN] = res.results[c]["yout"]
    return out


# version 13
# speedup vs baseline: 1.1373x; 1.1373x over previous
from contextlib import ExitStack

import numpy as np
import concourse.bass as bass
import concourse.mybir as mybir
from concourse.bass_utils import run_bass_kernel_spmd

F32 = mybir.dt.float32
F16 = mybir.dt.float16
ALU = mybir.AluOpType
ACTF = mybir.ActivationFunctionType
AX = mybir.AxisListType.X

B, S, D = 4, 2048, 2048
NH, DH, ROT = 32, 64, 32
HFF = 5632
NCORE = 8
TOK_OWN = S // 2
NT_OWN = TOK_OWN // 128
NT_ALL = S // 128
ND = D // 128
NG = 4
GH = 8
NHC = HFF // 128
YBLK = 4

DBG_GROUPS = NG
DBG_UNITS = 10 ** 9
DBG_TILES = NT_ALL
DBG_SERIAL = False
DBG_P5TILES = NT_OWN
DBG_NOROPE = False
DBG_NOTRANS = False
DBG_ROPLVL = 5

C_EXPSC, C_EPS, C_NHALF, C_INVD, C_THRSC, C_THRB, C_ONE, C_NEG1, C_ZERO = range(9)
NCONST = 16

ENGINES = ("sync", "gpsimd", "vector", "scalar", "tensor")


class Sem:
    def __init__(self, nc, name):
        self.h = nc.alloc_semaphore(name)
        self.v = 0


class Plan:

    def __init__(self):
        self.ops = {e: [] for e in ENGINES}

    def emit(self, eng, fn, wait=(), inc=()):
        self.ops[eng].append((list(wait), fn, list(inc)))
        for s, n in inc:
            s.v += n

    def pour(self, nc, name):
        with nc.Block(name) as blk:
            for ename in ENGINES:
                if not self.ops[ename]:
                    continue

                def body(e, ename=ename):
                    for waits, fn, incs in self.ops[ename]:
                        for s, v in waits:
                            e.wait_ge(s.h, v)
                        ins = fn(e)
                        for s, n in incs:
                            ins.then_inc(s.h, n)

                getattr(blk, ename)(body)


def _hc(ap_t, c):
    return ap_t.rearrange("p (h c) -> p h c", c=c)


def build(dbg=None):
    nc = bass.Bass("TRN2", target_bir_lowering=False, debug=False,
                   detect_race_conditions=False)

    xin = nc.dram_tensor("xin", [S, D], F32, kind="ExternalInput").ap()
    wqkv = nc.dram_tensor("wqkv", [D, 3 * D], F32, kind="ExternalInput").ap()
    bqkv = nc.dram_tensor("bqkv", [1, 3 * D], F32, kind="ExternalInput").ap()
    wout_in = nc.dram_tensor("wout", [D, D], F32, kind="ExternalInput").ap()
    bout = nc.dram_tensor("bout", [1, D], F32, kind="ExternalInput").ap()
    w1 = nc.dram_tensor("w1", [HFF, D], F32, kind="ExternalInput").ap()
    w2 = nc.dram_tensor("w2", [HFF, D], F32, kind="ExternalInput").ap()
    w3 = nc.dram_tensor("w3", [D, HFF], F32, kind="ExternalInput").ap()
    c1_in = nc.dram_tensor("c1", [S, 256], F16, kind="ExternalInput").ap()
    c2_in = nc.dram_tensor("c2", [S, 256], F16, kind="ExternalInput").ap()
    btbl_in = nc.dram_tensor("btbl", [128, 32], F32, kind="ExternalInput").ap()
    mband_in = nc.dram_tensor("mband", [128, 896], F16, kind="ExternalInput").ap()
    ident_in = nc.dram_tensor("ident", [128, 128], F16, kind="ExternalInput").ap()
    consts_in = nc.dram_tensor("consts", [128, NCONST], F32, kind="ExternalInput").ap()

    yout = nc.dram_tensor("yout", [TOK_OWN, D], F32, kind="ExternalOutput").ap()

    w1f = nc.dram_tensor("w1f", [HFF, D], F16, kind="Internal").ap()
    w2f = nc.dram_tensor("w2f", [HFF, D], F16, kind="Internal").ap()
    w3f = nc.dram_tensor("w3f", [D, HFF], F16, kind="Internal").ap()
    thr_hbm = nc.dram_tensor("thr_hbm", [1, 8], F32, kind="Internal").ap()

    dbg_outs = {}
    if dbg == "p1":
        dbg_outs["ln1"] = nc.dram_tensor("dbg_ln1", [ND * 128, S], F16, kind="ExternalOutput").ap()
    if dbg == "thr":
        dbg_outs["thr"] = nc.dram_tensor("dbg_thr", [128, 8], F32, kind="ExternalOutput").ap()
    if dbg == "p2":
        dbg_outs["qfm"] = nc.dram_tensor("dbg_qfm", [4 * 128, TOK_OWN], F16, kind="ExternalOutput").ap()
        dbg_outs["kfm"] = nc.dram_tensor("dbg_kfm", [4 * 128, S], F16, kind="ExternalOutput").ap()
        dbg_outs["vtm"] = nc.dram_tensor("dbg_vtm", [NT_ALL * 128, GH * 65], F16, kind="ExternalOutput").ap()
    if dbg == "p3":
        dbg_outs["aout"] = nc.dram_tensor("dbg_aout", [ND * 128, TOK_OWN], F16, kind="ExternalOutput").ap()
    if dbg == "p3u":
        dbg_outs["at"] = nc.dram_tensor("dbg_at", [2 * 128, 512], F16, kind="ExternalOutput").ap()
        dbg_outs["srow"] = nc.dram_tensor("dbg_srow", [1, 512], F32, kind="ExternalOutput").ap()
        dbg_outs["bc"] = nc.dram_tensor("dbg_bc", [64, 512], F32, kind="ExternalOutput").ap()
        dbg_outs["ao0"] = nc.dram_tensor("dbg_ao0", [128, TOK_OWN], F16, kind="ExternalOutput").ap()
    if dbg == "p5":
        dbg_outs["x2"] = nc.dram_tensor("dbg_x2", [TOK_OWN, D], F16, kind="ExternalOutput").ap()
        dbg_outs["ln3"] = nc.dram_tensor("dbg_ln3", [ND * 128, TOK_OWN], F16, kind="ExternalOutput").ap()
    if dbg == "p6":
        dbg_outs["y"] = nc.dram_tensor("dbg_y", [TOK_OWN, D], F16, kind="ExternalOutput").ap()
    if dbg == "fd":
        dbg_outs["kf0"] = nc.dram_tensor("dbg_kf0", [128, S], F16, kind="ExternalOutput").ap()
        dbg_outs["qf0"] = nc.dram_tensor("dbg_qf0", [128, TOK_OWN], F16, kind="ExternalOutput").ap()
        dbg_outs["vt0"] = nc.dram_tensor("dbg_vt0", [128, GH * 65], F16, kind="ExternalOutput").ap()
        dbg_outs["l10"] = nc.dram_tensor("dbg_l10", [128, S], F16, kind="ExternalOutput").ap()
        dbg_outs["l1t"] = nc.dram_tensor("dbg_l1t", [128, D], F16, kind="ExternalOutput").ap()
        dbg_outs["ao"] = nc.dram_tensor("dbg_ao", [ND * 128, TOK_OWN], F16, kind="ExternalOutput").ap()
        dbg_outs["x2d"] = nc.dram_tensor("dbg_x2d", [TOK_OWN, D], F16, kind="ExternalOutput").ap()
        dbg_outs["yd"] = nc.dram_tensor("dbg_yd", [TOK_OWN, D], F16, kind="ExternalOutput").ap()
        dbg_outs["y1s"] = nc.dram_tensor("dbg_y1s", [128, D], F32, kind="ExternalOutput").ap()
        dbg_outs["sts"] = nc.dram_tensor("dbg_sts", [128, 16], F32, kind="ExternalOutput").ap()
    if dbg == "p5a":
        dbg_outs["y1"] = nc.dram_tensor("dbg_y1", [128, D], F32, kind="ExternalOutput").ap()
        dbg_outs["st"] = nc.dram_tensor("dbg_st", [128, 16], F32, kind="ExternalOutput").ap()
        dbg_outs["x20"] = nc.dram_tensor("dbg_x20", [128, D], F16, kind="ExternalOutput").ap()
        dbg_outs["ao0"] = nc.dram_tensor("dbg_ao0", [128, TOK_OWN], F16, kind="ExternalOutput").ap()
        dbg_outs["wo0"] = nc.dram_tensor("dbg_wo0", [128, D], F16, kind="ExternalOutput").ap()

    A = nc.alloc_sbuf_tensor

    consts = A("consts_sb", [128, NCONST], F32)
    btbl = A("btbl_sb", [128, 32], F32)
    mband = A("mband_sb", [128, 896], F16)
    ident = A("ident_sb", [128, 128], F16)
    onesr = A("onesr", [1, 128], F16)
    onescol = A("onescol", [128, 1], F16)
    gacc = A("gacc", [128, 288], F32)
    gcol = A("gcol", [128, 4], F32)
    gcol16 = A("gcol16", [128, 4], F16)
    grow = A("grow", [1, 8], F32)
    thr = A("thr_sb", [128, 8], F32)
    stat = A("stat", [128, 32], F32)
    srow = A("srow", [1, 512], F32)
    rrow = A("rrow", [33, 512], F16)
    bcrow = A("bcrow", [64, 512], F16)

    aout = [A(f"aout{i}", [128, TOK_OWN], F16) for i in range(ND)]
    ln3_fm = aout
    ln1 = [A(f"ln1_{i}", [128, S], F16) for i in range(ND)]
    wout_sb = ln1

    pstk = ExitStack()
    pb = [pstk.enter_context(nc.psum_tensor(f"pb{i}", [128, 512], F32)) for i in range(6)]
    ptr = pstk.enter_context(nc.psum_tensor("ptr", [128, 512], F16))
    pbc = pstk.enter_context(nc.psum_tensor("pbc", [64, 512], F32))

    s_ld = Sem(nc, "s_ld")
    s_xp = [Sem(nc, "s_x0"), Sem(nc, "s_x1")]
    s_cp = [Sem(nc, "s_c0"), Sem(nc, "s_c1")]
    s_wk = Sem(nc, "s_wk")
    s_gp = [Sem(nc, f"s_g{i}") for i in range(4)]
    s_gbp = [Sem(nc, f"s_gb{i}") for i in range(4)]
    s_gr = Sem(nc, "s_gr")
    s_gv2 = Sem(nc, "s_gv2")
    s_mm = Sem(nc, "s_mm")
    s_ve = Sem(nc, "s_ve")
    s_se = Sem(nc, "s_se")
    s_stp = [Sem(nc, "s_st0"), Sem(nc, "s_st1")]
    s_fw12 = [Sem(nc, "s_fw12a"), Sem(nc, "s_fw12b")]
    s_fw3 = [Sem(nc, "s_fw3a"), Sem(nc, "s_fw3b")]
    s_thr = Sem(nc, "s_thr")
    s_dbg = Sem(nc, "s_dbg")

    p = Plan()
    p.emit("sync", lambda e: e.dma_start(out=consts[:, :], in_=consts_in[:, :]), inc=[(s_ld, 16)])
    p.emit("sync", lambda e: e.dma_start(out=btbl[:, :], in_=btbl_in[:, :]), inc=[(s_ld, 16)])
    p.emit("sync", lambda e: e.dma_start(out=ident[:, :], in_=ident_in[:, :]), inc=[(s_ld, 16)])
    p.emit("sync", lambda e: e.dma_start(out=mband[:, :], in_=mband_in[:, :]), inc=[(s_ld, 16)])
    p.emit("vector", lambda e: e.memset(onesr[:, :], 1.0), inc=[(s_ve, 1)])
    p.emit("vector", lambda e: e.memset(onescol[:, :], 1.0), inc=[(s_ve, 1)])
    p.emit("vector", lambda e: e.memset(grow[:, :], 0.0), inc=[(s_ve, 1)])
    p.emit("vector", lambda e: e.memset(stat[:, :], 0.0), inc=[(s_ve, 1)])
    p.pour(nc, "p0")
    VE0 = s_ve.v

    gld, gbd = {}, {}
    gchunks = []
    for wi, (src, dst) in enumerate(((w1, w1f), (w2, w2f), (w3, w3f))):
        ncols = src.shape[1]
        for ch in range(src.shape[0] // 128):
            for c0 in range(0, ncols, 1024):
                gchunks.append((wi, src, dst, ch, c0, min(1024, ncols - c0)))
    NGC = len(gchunks)
    GOFFS = []
    for wi in range(3):
        idx = [i for i, gc in enumerate(gchunks) if gc[0] == wi]
        GOFFS.append((min(idx), len(idx)))
    gst = {"ld": 0}

    def gamma_pump(p, n, gbuf):
        def red(i):
            wi, src, dst, ch, c0, cn = gchunks[i]
            buf = gbuf[i % 3]
            sg = s_gp[i % 3]
            p.emit("sync",
                   lambda e, buf=buf, dst=dst, ch=ch, c0=c0, cn=cn:
                       e.dma_start(out=dst[128 * ch:128 * ch + 128, c0:c0 + cn], in_=buf[:, 0:cn]),
                   wait=[(sg, gld[i])], inc=[(s_gbp[i % 3], 16)])
            gbd[i] = s_gbp[i % 3].v
            p.emit("vector",
                   lambda e, buf=buf, i=i, cn=cn:
                       e.reduce_sum(gacc[:, i:i + 1], buf[:, 0:cn], axis=AX, apply_absolute_value=True),
                   wait=[(sg, gld[i])], inc=[(s_gr, 1)])

        end = min(gst["ld"] + n, NGC)
        while gst["ld"] < end:
            i = gst["ld"]
            wi, src, dst, ch, c0, cn = gchunks[i]
            buf = gbuf[i % 3]
            sg = s_gp[i % 3]
            w = [(s_gr, i - 2), (s_gbp[i % 3], gbd[i - 3])] if i >= 3 else []
            p.emit("gpsimd",
                   lambda e, buf=buf, src=src, ch=ch, c0=c0, cn=cn:
                       e.dma_start(out=buf[:, 0:cn], in_=src[128 * ch:128 * ch + 128, c0:c0 + cn]),
                   wait=w, inc=[(sg, 16)])
            gld[i] = sg.v
            if i >= 2:
                red(i - 2)
            gst["ld"] = i + 1
            if i == NGC - 1:
                red(NGC - 2)
                red(NGC - 1)

    def gamma_finish(p):
        offs = GOFFS

        def red(e):
            ins = None
            for wi2, (o, nn) in enumerate(offs):
                ins = e.reduce_sum(gcol[:, wi2:wi2 + 1], gacc[:, o:o + nn], axis=AX)
            return ins
        p.emit("vector", red, wait=[(s_gr, NGC)], inc=[(s_ve, 1)])
        p.emit("vector", lambda e: e.tensor_scalar_mul(gcol16[:, 0:3], gcol[:, 0:3], 1.0 / 65536.0),
               wait=[(s_ve, s_ve.v)], inc=[(s_ve, 1)])
        p.emit("tensor", lambda e: e.matmul(pb[5][0:1, 0:3], onescol[:, 0:1], gcol16[:, 0:3],
                                            start=True, stop=True),
               wait=[(s_ve, s_ve.v)], inc=[(s_mm, 1)])
        p.emit("vector", lambda e: e.tensor_scalar(out=grow[0:1, 0:3], in0=pb[5][0:1, 0:3],
                                                   scalar1=65536.0 * 0.5 / (HFF * D), scalar2=0.5e-8,
                                                   op0=ALU.mult, op1=ALU.add),
               wait=[(s_mm, s_mm.v)], inc=[(s_ve, 1)])
        p.emit("vector", lambda e: e.tensor_scalar_mul(grow[0:1, 4:7], grow[0:1, 0:3], -1.0),
               wait=[(s_ve, s_ve.v)], inc=[(s_ve, 1)])
        p.emit("sync", lambda e: e.dma_start(out=thr_hbm[0:1, 0:8], in_=grow[0:1, 0:8]),
               wait=[(s_ve, s_ve.v)], inc=[(s_gbp[0], 16)])
        p.emit("sync", lambda e: e.dma_start(out=thr[:, 0:8], in_=thr_hbm[0, 0:8].partition_broadcast(128)),
               wait=[(s_gbp[0], s_gbp[0].v)], inc=[(s_thr, 16)])

    stkg = ExitStack()
    gbuf = [stkg.enter_context(nc.sbuf_tensor(f"gbuf{i}", [128, 1024], F16)) for i in range(3)]
    stk1 = ExitStack()
    E1 = lambda name, shape, dt: stk1.enter_context(nc.sbuf_tensor(name, shape, dt))
    ln1tm = [E1(f"ln1tm{i}", [128, D], F16) for i in range(2)]
    xbuf = [E1(f"xbuf{i}", [128, D], F32) for i in range(2)]
    scr0 = E1("p1scr", [128, 2048], F32)
    scr1p = [E1(f"p1scrb{i}", [128, 1024], F32) for i in range(2)]

    vstamp, mmstamp, sqred, normst = {}, {}, {}, {}
    sestamp = {"tr": 0}

    p = Plan()
    p.emit("vector", lambda e: e.memset(scr0[:, :], 0.0), inc=[(s_ve, 1)])
    p.emit("vector", lambda e: e.memset(scr1p[0][:, :], 0.0), inc=[(s_ve, 1)])
    p.emit("vector", lambda e: e.memset(scr1p[1][:, :], 0.0), inc=[(s_ve, 1)])
    MEMS0 = s_ve.v

    def p1_s1(tt):
        sb = 8 * (tt % 2)
        xb = xbuf[tt % 2]
        w = [(s_ve, vstamp[tt - 2])] if tt >= 2 else []
        sx = s_xp[tt % 2]
        p.emit("sync", lambda e, xb=xb, tt=tt: e.dma_start(out=xb[:, :], in_=xin[128 * tt:128 * tt + 128, :]),
               wait=w, inc=[(sx, 16)])
        wv = [(sx, sx.v)]
        if tt == 0:
            wv.append((s_ld, s_ld.v))
        p.emit("vector", lambda e, xb=xb, sb=sb: e.reduce_sum(stat[:, sb:sb + 1], xb[:, :], axis=AX), wait=wv)
        wsq = [(sx, sx.v)]
        if tt >= 2:
            wsq.append((s_ve, sqred[tt - 2]))
        else:
            wsq.append((s_ve, MEMS0))
        p.emit("scalar", lambda e, xb=xb, tt=tt: e.square(scr0[:, 1024 * (tt % 2):1024 * (tt % 2) + 1024],
                                                          xb[:, 0:1024]),
               wait=wsq, inc=[(s_se, 1)])
        p.emit("scalar", lambda e, xb=xb, tt=tt: e.square(scr1p[tt % 2][:, :], xb[:, 1024:2048]),
               inc=[(s_se, 1)])
        p.emit("vector", lambda e, sb=sb, tt=tt: e.reduce_sum(stat[:, sb + 1:sb + 2],
                                                              scr0[:, 1024 * (tt % 2):1024 * (tt % 2) + 1024],
                                                              axis=AX),
               wait=[(s_se, s_se.v)], inc=[(s_ve, 1)])
        p.emit("vector", lambda e, sb=sb, tt=tt: e.reduce_sum(stat[:, sb + 7:sb + 8], scr1p[tt % 2][:, :], axis=AX),
               wait=[(s_ve, s_ve.v)], inc=[(s_ve, 1)])
        sqred[tt] = s_ve.v
        p.emit("vector", lambda e, sb=sb: e.tensor_add(stat[:, sb + 1:sb + 2], stat[:, sb + 1:sb + 2],
                                                       stat[:, sb + 7:sb + 8]),
               wait=[(s_ve, s_ve.v)], inc=[(s_ve, 1)])
        p.emit("vector", lambda e, sb=sb: e.tensor_scalar_mul(stat[:, sb + 2:sb + 3], stat[:, sb:sb + 1], 1.0 / D),
               wait=[(s_ve, s_ve.v)], inc=[(s_ve, 1)])
        p.emit("vector", lambda e, sb=sb: e.tensor_scalar_mul(stat[:, sb + 3:sb + 4], stat[:, sb + 1:sb + 2], 1.0 / D),
               wait=[(s_ve, s_ve.v)], inc=[(s_ve, 1)])
        p.emit("vector", lambda e, sb=sb: e.scalar_tensor_tensor(out=stat[:, sb + 4:sb + 5], in0=stat[:, sb + 2:sb + 3],
                                                                 scalar=1.0, in1=stat[:, sb + 2:sb + 3],
                                                                 op0=ALU.mult, op1=ALU.mult),
               wait=[(s_ve, s_ve.v)], inc=[(s_ve, 1)])
        p.emit("vector", lambda e, sb=sb: e.tensor_sub(stat[:, sb + 5:sb + 6], stat[:, sb + 3:sb + 4],
                                                       stat[:, sb + 4:sb + 5]),
               wait=[(s_ve, s_ve.v)], inc=[(s_ve, 1)])
        p.emit("scalar", lambda e, sb=sb: e.activation(out=stat[:, sb + 7:sb + 8], in_=stat[:, sb + 5:sb + 6],
                                                       func=ACTF.Ln, bias=consts[:, C_EPS:C_EPS + 1],
                                                       scale=consts[:, C_ONE:C_ONE + 1]),
               wait=[(s_ve, s_ve.v)], inc=[(s_se, 1)])
        p.emit("scalar", lambda e, sb=sb: e.activation(out=stat[:, sb + 6:sb + 7], in_=stat[:, sb + 7:sb + 8],
                                                       func=ACTF.Exp, bias=consts[:, C_ZERO:C_ZERO + 1],
                                                       scale=consts[:, C_NHALF:C_NHALF + 1]),
               wait=[(s_se, s_se.v)], inc=[(s_se, 1)])
        rstamp[tt] = s_se.v

    def p1_s2(tt):
        sb = 8 * (tt % 2)
        xb = xbuf[tt % 2]
        lt = ln1tm[tt % 2]
        wapply = [(s_se, rstamp[tt])]
        if tt >= 2:
            wapply.append((s_mm, mmstamp[tt - 2]))
        p.emit("vector", lambda e, xb=xb, lt=lt, sb=sb: e.tensor_scalar(
            out=lt[:, :], in0=xb[:, :], scalar1=stat[:, sb + 2:sb + 3], scalar2=stat[:, sb + 6:sb + 7],
            op0=ALU.subtract, op1=ALU.mult),
               wait=wapply, inc=[(s_ve, 1)])
        vstamp[tt] = s_ve.v
        for fq in range(4):
            def trans(e, lt=lt, fq=fq):
                ins = None
                for i in range(4):
                    dc = 4 * fq + i
                    ins = e.transpose(ptr[:, 128 * i:128 * i + 128],
                                      lt[:, 128 * dc:128 * dc + 128], ident[:, :])
                return ins
            wlist = [(s_ve, s_ve.v)]
            if (tt, fq) == (0, 0):
                wlist.append((s_ld, s_ld.v))
            else:
                wlist.append((s_se, sestamp["tr"]))
            p.emit("tensor", trans, wait=wlist, inc=[(s_mm, 1)])

            def evac(e, tt=tt, fq=fq):
                ins = None
                for i in range(4):
                    dc = 4 * fq + i
                    ins = e.copy(ln1[dc][:, 128 * tt:128 * tt + 128], ptr[:, 128 * i:128 * i + 128])
                return ins
            p.emit("scalar", evac, wait=[(s_mm, s_mm.v)], inc=[(s_se, 1)])
            sestamp["tr"] = s_se.v
        mmstamp[tt] = s_mm.v

    rstamp = {}
    for tt in range(NT_ALL + 1):
        if tt < NT_ALL:
            p1_s1(tt)
        if tt >= 1:
            p1_s2(tt - 1)
        gamma_pump(p, 1, gbuf)
    if dbg == "thr":
        gamma_pump(p, NGC, gbuf)
        gamma_finish(p)
    p.pour(nc, "p1")

    if dbg in ("p1", "thr"):
        p = Plan()
        if dbg == "p1":
            for dc in range(ND):
                p.emit("sync", lambda e, dc=dc: e.dma_start(
                    out=dbg_outs["ln1"][128 * dc:128 * dc + 128, :], in_=ln1[dc][:, :]), inc=[(s_dbg, 16)])
        else:
            p.emit("sync", lambda e: e.dma_start(out=dbg_outs["thr"][:, :], in_=thr[:, :]),
                   wait=[(s_thr, 16)], inc=[(s_dbg, 16)])
        p.emit("sync", lambda e: e.nop(), wait=[(s_dbg, s_dbg.v)])
        p.pour(nc, "dbgf")
        return nc

    stk1.close()

    stk23 = ExitStack()
    E23 = lambda name, shape, dt: stk23.enter_context(nc.sbuf_tensor(name, shape, dt))
    wqres = [E23(f"wqres{i}", [128, 1536], F16) for i in range(ND)]
    cbuf = [E23(f"cbuf{i}", [128, 512], F16) for i in range(2)]
    qfm = [E23(f"qfm{i}", [128, TOK_OWN], F16) for i in range(4)]
    kfm = [E23(f"kfm{i}", [128, S], F16) for i in range(4)]
    vtm = [E23(f"vtm{i}", [128, GH * 65], F16) for i in range(NT_ALL)]
    attnT = [E23(f"attnT{i}", [128, 512], F16) for i in range(3)]
    qk_tm = [attnT[0], attnT[1]]
    bqg = E23("bqg", [1, 3 * 512], F16)
    rsa = [E23("rsa0", [128, 256], F16)] * 2
    rsb = [E23("rsb0", [128, 256], F16)] * 2

    p = Plan()

    def _memset_vtm(e):
        ins = None
        for i in range(NT_ALL):
            ins = e.memset(vtm[i][:, :], 1.0)
        return ins
    p.emit("vector", _memset_vtm, inc=[(s_ve, 1)])
    p.pour(nc, "vns")
    VTM_VE = s_ve.v

    last_qkv_mm = {(-1): 0}
    wq_loaded = {}

    for g in range(min(NG, max(DBG_GROUPS, 1))):
        p = Plan()
        if g == 0:
            for dc in range(ND):
                p.emit("gpsimd", lambda e, dc=dc: e.dma_start(
                    out=wqres[dc][:, :], in_=wqkv[128 * dc:128 * dc + 128, 0:1536]),
                       inc=[(s_wk, 16)])
            p.emit("gpsimd", lambda e: e.dma_start(out=bqg[0:1, :], in_=bqkv[0:1, 0:1536]),
                   inc=[(s_wk, 16)])
            wq_loaded[0] = s_wk.v

        wk_all = wq_loaded[g]
        evst = {}
        qk_trans = {}

        cb_free = {}
        cst = {}
        mmq = {}

        def p2_s1(t):
            subs = (0, 1, 2) if t < NT_OWN else (1, 2)
            banks = [pb[3 * (t % 2) + i] for i in range(3)]
            wcb = [(s_ve, cb_free[t - 2])] if t >= 2 else []
            sc = s_cp[t % 2]
            p.emit("sync", lambda e, t=t: e.dma_start(out=cbuf[t % 2][:, 0:256],
                                                      in_=c1_in[128 * t:128 * t + 128, :]),
                   wait=wcb, inc=[(sc, 16)])
            p.emit("sync", lambda e, t=t: e.dma_start(out=cbuf[t % 2][:, 256:512],
                                                      in_=c2_in[128 * t:128 * t + 128, :]),
                   inc=[(sc, 16)])
            cst[t] = (sc, sc.v)
            wmm = [(s_wk, wk_all)]
            if t >= 2:
                wmm += [(s_ve, evst[t - 2][0]), (s_se, evst[t - 2][1])]
            if g == 0 and t == 0:
                wmm.append((s_ve, VTM_VE))

            def qkv_mms(e, t=t, subs=subs, banks=banks):
                ins = None
                for dc in range(ND):
                    for sub in subs:
                        ins = e.matmul(banks[sub][:, 0:512], ln1[dc][:, 128 * t:128 * t + 128],
                                       wqres[dc][:, 512 * sub:512 * sub + 512],
                                       start=(dc == 0), stop=False)
                for sub in subs:
                    ins = e.matmul(banks[sub][:, 0:512], onesr[0:1, 0:128],
                                   bqg[0:1, 512 * sub:512 * sub + 512], start=False, stop=True)
                return ins
            p.emit("tensor", qkv_mms, wait=wmm, inc=[(s_mm, 1)])
            mmq[t] = s_mm.v

        def p2_s2(t):
            subs = (0, 1, 2) if t < NT_OWN else (1, 2)
            banks = [pb[3 * (t % 2) + i] for i in range(3)]
            mm_qkv = mmq[t]
            c_sem, c_stamp = cst[t]
            p.emit("vector", lambda e, t=t, bk=banks[2]: e.tensor_copy(
                _hc(vtm[t][:, :], 65)[:, :, 0:64], _hc(bk[:, 0:512], 64)),
                   wait=[(s_mm, mm_qkv)], inc=[(s_ve, 1)])
            v_ve = s_ve.v
            ve_rope_last = 0
            for sub in subs[:-1]:
                bk = banks[sub]
                sa, sb2 = rsa[sub], rsb[sub]
                qt = qk_tm[t % 2]
                wr0 = [(s_mm, mm_qkv), (c_sem, c_stamp)]
                if sub == 1 and 0 in subs:
                    wr0.append((s_mm, qk_trans[(t, 0)]))
                elif t >= 2:
                    wr0.append((s_mm, qk_trans[(t - 2, (0, 1, 2)[-2] if t - 2 < NT_OWN else 1)]))
                p.emit("vector", lambda e, bk=bk, sa=sa, t=t: e.tensor_mul(
                    _hc(sa, 32)[:, :, 0:16], _hc(bk[:, 0:512], 64)[:, :, 16:32],
                    _hc(cbuf[t % 2][:, 256:512], 32)[:, :, 0:16]),
                       wait=wr0)
                p.emit("vector", lambda e, bk=bk, sa=sa, t=t: e.tensor_mul(
                    _hc(sa, 32)[:, :, 16:32], _hc(bk[:, 0:512], 64)[:, :, 0:16],
                    _hc(cbuf[t % 2][:, 256:512], 32)[:, :, 16:32]))
                p.emit("vector", lambda e, bk=bk, sb2=sb2, t=t: e.tensor_mul(
                    _hc(sb2, 32), _hc(bk[:, 0:512], 64)[:, :, 0:32],
                    _hc(cbuf[t % 2][:, 0:256], 32)),
                       inc=[(s_ve, 1)])
                p.emit("vector", lambda e, sa=sa, sb2=sb2, qt=qt: e.tensor_add(
                    _hc(qt, 64)[:, :, 0:32], _hc(sb2, 32), _hc(sa, 32)),
                       inc=[(s_ve, 1)])
                p.emit("vector", lambda e, bk=bk, qt=qt: e.tensor_copy(
                    _hc(qt, 64)[:, :, 32:64], _hc(bk[:, 0:512], 64)[:, :, 32:64]),
                       inc=[(s_ve, 1)])
                ve_rope = s_ve.v
                dst = qfm if sub == 0 else kfm

                def trans(e, qt=qt):
                    ins = None
                    for i in range(4):
                        ins = e.transpose(ptr[:, 128 * i:128 * i + 128],
                                          qt[:, 128 * i:128 * i + 128], ident[:, :])
                    return ins
                p.emit("tensor", trans,
                       wait=[(s_ve, ve_rope), (s_se, sestamp["tr"])], inc=[(s_mm, 1)])
                qk_trans[(t, sub)] = s_mm.v

                def evac(e, t=t, dst=dst):
                    ins = None
                    for i in range(4):
                        ins = e.copy(dst[i][:, 128 * t:128 * t + 128], ptr[:, 128 * i:128 * i + 128])
                    return ins
                p.emit("scalar", evac, wait=[(s_mm, s_mm.v)], inc=[(s_se, 1)])
                sestamp["tr"] = s_se.v
                ve_rope_last = ve_rope
            evst[t] = (max(ve_rope_last, v_ve), sestamp["tr"])
            cb_free[t] = evst[t][0]

        for t in range(NT_ALL + 1):
            if t < NT_ALL:
                p2_s1(t)
            if t >= 1:
                p2_s2(t - 1)
            gamma_pump(p, 1, gbuf)
        last_qkv_mm[g] = s_mm.v
        group_ready_se = s_se.v
        group_ready_ve = s_ve.v

        if g + 1 < min(NG, max(DBG_GROUPS, 1)):
            for dc in range(ND):
                wld = [(s_mm, last_qkv_mm[g])] if dc == 0 else []
                p.emit("gpsimd", lambda e, dc=dc, g=g: e.dma_start(
                    out=wqres[dc][:, :], in_=wqkv[128 * dc:128 * dc + 128, 1536 * (g + 1):1536 * (g + 1) + 1536]),
                       wait=wld, inc=[(s_wk, 16)])
            p.emit("gpsimd", lambda e, g=g: e.dma_start(out=bqg[0:1, :],
                                                        in_=bqkv[0:1, 1536 * (g + 1):1536 * (g + 1) + 1536]),
                   wait=[(s_mm, last_qkv_mm[g])], inc=[(s_wk, 16)])
            wq_loaded[g + 1] = s_wk.v

        if g == NG - 1 and dbg != "p2" and dbg != "p3" and dbg != "p3u" and DBG_GROUPS >= NG:
            for ec in range(ND):
                wld = [(s_mm, last_qkv_mm[g])] if ec == 0 else []
                p.emit("gpsimd", lambda e, ec=ec: e.dma_start(out=wout_sb[ec][:, 0:D],
                                                              in_=wout_in[128 * ec:128 * ec + 128, :]),
                       wait=wld, inc=[(s_wk, 16)])
            wq_loaded["wout"] = s_wk.v

        obm = {0: pb[4], 1: pb[5]}
        avq = {}

        def emit_denoms(hu2):
            for st in (0, 1):
                p.emit("scalar", lambda e, st=st: e.copy(srow[0:1, 0:512], obm[st][64:65, 0:512]),
                       wait=[(s_mm, avq[hu2][st])], inc=[(s_se, 1)])

                def dln(e):
                    with nc.allow_low_precision("softmax denominators via fp16 exp(-ln): row-"
                                                "uniform scale that LN2 cancels"):
                        return e.activation(out=rrow[32:33, 0:512], in_=srow[0:1, 0:512],
                                            func=ACTF.Ln, bias=consts[32:33, C_ZERO:C_ZERO + 1],
                                            scale=consts[32:33, C_ONE:C_ONE + 1])
                p.emit("scalar", dln, wait=[(s_se, s_se.v)], inc=[(s_se, 1)])

                def dexp(e):
                    with nc.allow_low_precision("softmax denominators via fp16 exp(-ln)"):
                        return e.activation(out=rrow[0:1, 0:512], in_=rrow[32:33, 0:512],
                                            func=ACTF.Exp, bias=consts[0:1, C_ZERO:C_ZERO + 1],
                                            scale=consts[0:1, C_NEG1:C_NEG1 + 1])
                p.emit("scalar", dexp, wait=[(s_se, s_se.v), (s_mm, mmstamp.get("bcast", 0))],
                       inc=[(s_se, 1)])
                p.emit("tensor", lambda e: e.matmul(pbc[0:64, 0:512], onesr[0:1, 0:64],
                                                    rrow[0:1, 0:512], start=True, stop=True),
                       wait=[(s_se, s_se.v)], inc=[(s_mm, 1)])
                mmstamp["bcast"] = s_mm.v
                p.emit("scalar", lambda e: e.copy(bcrow[0:64, 0:512], pbc[0:64, 0:512]),
                       wait=[(s_mm, s_mm.v), (s_ve, vstamp.get("norm", 0))], inc=[(s_se, 1)])
                p.emit("vector", lambda e, st=st, hu2=hu2: e.tensor_mul(
                    aout[hu2 // 2][64 * (hu2 % 2):64 * (hu2 % 2) + 64, 512 * st:512 * st + 512],
                    obm[st][0:64, 0:512], bcrow[0:64, 0:512]),
                       wait=[(s_se, s_se.v)], inc=[(s_ve, 1)])
                vstamp["norm"] = s_ve.v
                normst[(hu2, st)] = s_ve.v

        for h in range(GH):
            if g >= DBG_GROUPS or 2 * (GH * g + h) >= DBG_UNITS:
                break
            fc, r0 = h // 2, 64 * (h % 2)
            hu = GH * g + h
            L = [(j, st) for j in range(16) for st in (0, 1) if (j < 4 * st + 4 or j >= 8)]
            ob = {0: pb[4], 1: pb[5]}
            first_st = {st: min(i for i, (jj, ss) in enumerate(L) if ss == st) for st in (0, 1)}
            last_st = {st: max(i for i, (jj, ss) in enumerate(L) if ss == st) for st in (0, 1)}
            exp_st, av_st, msk_st = {}, {}, {}
            prev_h_exp = sestamp.get("unit_exp", 0)
            prev_h_av = mmstamp.get("unit_av", 0)
            NCHI = len(L)

            def emit_scores(ci):
                j, st = L[ci]
                sbk = pb[ci % 4]
                wsc = []
                if hu == GH * g and ci < 4:
                    wsc.append((s_se, group_ready_se))
                    wsc.append((s_ve, group_ready_ve))
                if ci >= 4:
                    wsc.append((s_se, exp_st[ci - 4]))
                elif hu > 0:
                    wsc.append((s_se, prev_h_exp))
                p.emit("tensor", lambda e, sbk=sbk, j=j, st=st, fc=fc, r0=r0: e.matmul(
                    sbk[:, 0:512], kfm[fc][r0:r0 + 64, 128 * j:128 * j + 128],
                    qfm[fc][r0:r0 + 64, 512 * st:512 * st + 512], start=True, stop=True),
                       wait=wsc, inc=[(s_mm, 1)])
                sc_mm = s_mm.v
                at = attnT[ci % 3]
                wex = [(s_mm, sc_mm)]
                if ci >= 3:
                    wex.append((s_mm, av_st[ci - 3]))
                elif hu > 0:
                    wex.append((s_mm, prev_h_av))
                p.emit("scalar", lambda e, sbk=sbk, at=at, st=st, j=j: e.activation(
                    out=at[:, 0:512], in_=sbk[:, 0:512], func=ACTF.Exp,
                    bias=btbl[:, 16 * st + j:16 * st + j + 1],
                    scale=consts[:, C_EXPSC:C_EXPSC + 1]),
                       wait=wex, inc=[(s_se, 1)])
                exp_st[ci] = s_se.v
                if 4 * st <= j < 4 * st + 4:
                    m = j - 4 * st
                    p.emit("vector", lambda e, at=at, m=m: e.tensor_mul(
                        at[:, 0:512], at[:, 0:512], mband[:, 384 - 128 * m:896 - 128 * m]),
                           wait=[(s_se, exp_st[ci])], inc=[(s_ve, 1)])
                    msk_st[ci] = s_ve.v

            def emit_av(ci):
                j, st = L[ci]
                at = attnT[ci % 3]
                wav = [(s_ve, msk_st[ci])] if ci in msk_st else [(s_se, exp_st[ci])]
                if ci == first_st[st] and hu >= 1:
                    wav.append((s_ve, normst[(hu - 1, st)]))
                    if hu == GH * g:
                        wav.append((s_ve, group_ready_ve))
                elif ci == first_st[st] and hu == 0:
                    wav.append((s_ve, group_ready_ve))
                p.emit("tensor",
                       lambda e, st=st, j=j, at=at, h=h, first=(ci == first_st[st]), last=(ci == last_st[st]):
                           e.matmul(ob[st][0:65, 0:512], vtm[j][:, 65 * h:65 * h + 65], at[:, 0:512],
                                    start=first, stop=last),
                       wait=wav, inc=[(s_mm, 1)])
                av_st[ci] = s_mm.v

            LAG = 3
            for ci in range(NCHI):
                if ci == LAG and hu > GH * g:
                    emit_denoms(hu - 1)
                if ci >= LAG:
                    emit_av(ci - LAG)
                emit_scores(ci)
                if ci in (10, 20):
                    gamma_pump(p, 3, gbuf)
            for ci in range(NCHI - LAG, NCHI):
                emit_av(ci)
            sestamp["unit_exp"] = exp_st[NCHI - 1]
            mmstamp["unit_av"] = av_st[NCHI - 1]
            avq[hu] = {st: av_st[last_st[st]] for st in (0, 1)}

        if avq:
            emit_denoms(max(avq))
        if dbg == "fd" and g == 0:
            p.emit("sync", lambda e: e.dma_start(out=dbg_outs["kf0"][:, :], in_=kfm[0][:, :]),
                   wait=[(s_se, group_ready_se)], inc=[(s_dbg, 16)])
            p.emit("sync", lambda e: e.dma_start(out=dbg_outs["qf0"][:, :], in_=qfm[0][:, :]),
                   inc=[(s_dbg, 16)])
            p.emit("sync", lambda e: e.dma_start(out=dbg_outs["vt0"][:, :], in_=vtm[0][:, :]),
                   inc=[(s_dbg, 16)])
            p.emit("sync", lambda e: e.dma_start(out=dbg_outs["l10"][:, :], in_=ln1[0][:, :]),
                   inc=[(s_dbg, 16)])
            p.emit("sync", lambda e: e.dma_start(out=dbg_outs["l1t"][:, :], in_=ln1tm[0][:, :])
                   if False else e.nop(), inc=[])
        if g == min(NG, max(DBG_GROUPS, 1)) - 1 and dbg != "thr":
            gamma_pump(p, NGC, gbuf)
            gamma_finish(p)
        p.pour(nc, f"p23_{g}")

    nc._dbg = dict(qk_tm=qk_tm, rsa=rsa, rsb=rsb, cbuf=cbuf, qfm=qfm, kfm=kfm,
                   vtm=vtm, ln1=ln1, aout=aout)
    if dbg == "p3u":
        p = Plan()
        for i in range(2):
            p.emit("sync", lambda e, i=i: e.dma_start(out=dbg_outs["at"][128 * i:128 * i + 128, :],
                                                      in_=attnT[i][:, :]), inc=[(s_dbg, 16)])
        p.emit("sync", lambda e: e.dma_start(out=dbg_outs["srow"][:, :], in_=srow[:, :]), inc=[(s_dbg, 16)])
        p.emit("sync", lambda e: e.dma_start(out=dbg_outs["bc"][:, :], in_=bcrow[:, :]), inc=[(s_dbg, 16)])
        p.emit("sync", lambda e: e.dma_start(out=dbg_outs["ao0"][:, :], in_=aout[0][:, :]), inc=[(s_dbg, 16)])
        p.emit("sync", lambda e: e.nop(), wait=[(s_dbg, s_dbg.v), (s_thr, 16)])
        p.pour(nc, "dbgf")
        return nc

    if dbg == "p2":
        p = Plan()
        for i in range(4):
            p.emit("sync", lambda e, i=i: e.dma_start(out=dbg_outs["qfm"][128 * i:128 * i + 128, :],
                                                      in_=qfm[i][:, :]), inc=[(s_dbg, 16)])
            p.emit("sync", lambda e, i=i: e.dma_start(out=dbg_outs["kfm"][128 * i:128 * i + 128, :],
                                                      in_=kfm[i][:, :]), inc=[(s_dbg, 16)])
        for t in range(NT_ALL):
            p.emit("sync", lambda e, t=t: e.dma_start(out=dbg_outs["vtm"][128 * t:128 * t + 128, :],
                                                      in_=vtm[t][:, :]), inc=[(s_dbg, 16)])
        p.emit("sync", lambda e: e.nop(), wait=[(s_dbg, s_dbg.v), (s_thr, 16)])
        p.pour(nc, "dbgf")
        return nc

    if dbg == "p3":
        p = Plan()
        for i in range(ND):
            p.emit("sync", lambda e, i=i: e.dma_start(out=dbg_outs["aout"][128 * i:128 * i + 128, :],
                                                      in_=aout[i][:, :]), inc=[(s_dbg, 16)])
        p.emit("sync", lambda e: e.nop(), wait=[(s_dbg, s_dbg.v), (s_thr, 16)])
        p.pour(nc, "dbgf")
        return nc

    stk23.close()
    stkg.close()

    stk5o = ExitStack()
    E5o = lambda name, shape, dt: stk5o.enter_context(nc.sbuf_tensor(name, shape, dt))
    x2 = [E5o(f"x2_{t}", [128, D], F16) for t in range(NT_OWN)]
    scr5 = E5o("scr5", [128, 2048], F32)
    bout_sb = E5o("bout_sb", [1, D], F16)

    stk5 = ExitStack()
    E5 = lambda name, shape, dt: stk5.enter_context(nc.sbuf_tensor(name, shape, dt))
    xb5 = [E5(f"xb5_{i}", [128, D], F32) for i in range(3)]
    y1 = [E5(f"y1_{i}", [128, 2048], F32) for i in range(2)]
    l3t = [E5(f"l3t{i}", [128, D], F16) for i in range(2)]
    scr5b = E5("scr5b", [128, 2048], F32)
    scr5p = [scr5, scr5b]

    p = Plan()
    p.emit("vector", lambda e: e.memset(scr5[:, :], 0.0), inc=[(s_ve, 1)])
    p.emit("vector", lambda e: e.memset(scr5b[:, :], 0.0), inc=[(s_ve, 1)])
    p.emit("vector", lambda e: e.memset(y1[0][:, :], 0.0), inc=[(s_ve, 1)])
    p.emit("vector", lambda e: e.memset(y1[1][:, :], 0.0), inc=[(s_ve, 1)])
    if dbg == "fd":
        for i in range(ND):
            p.emit("sync", lambda e, i=i: e.dma_start(out=dbg_outs["ao"][128 * i:128 * i + 128, :],
                                                      in_=aout[i][:, :]), inc=[(s_dbg, 16)])
        AO_DUMP = s_dbg.v
    p.emit("gpsimd", lambda e: e.dma_start(out=bout_sb[:, :], in_=bout[:, :]), inc=[(s_wk, 16)])
    if "wout" not in wq_loaded:
        for ec in range(ND):
            p.emit("gpsimd", lambda e, ec=ec: e.dma_start(out=wout_sb[ec][:, 0:D],
                                                          in_=wout_in[128 * ec:128 * ec + 128, :]),
                   inc=[(s_wk, 16)])
    wout_wk = s_wk.v

    p5_evst = {}
    x2add = {}
    opst = {}
    l3st = {}

    def p5_mm(t):
        sx5 = s_xp[t % 2]
        p.emit("sync", lambda e, t=t: e.dma_start(out=xb5[t % 3][:, :], in_=xin[128 * t:128 * t + 128, :]),
               wait=([(s_ve, x2add[t - 3])] if t >= 3 else []), inc=[(sx5, 16)])
        x2add[("sem", t)] = (sx5, sx5.v)
        wmm = [(s_wk, wout_wk)]
        if t >= 1:
            wmm.append((s_ve, p5_evst[t - 1]))

        def op_mms(e, t=t):
            ins = None
            for dg in range(4):
                for ec in range(ND):
                    ins = e.matmul(pb[dg][:, 0:512], aout[ec][:, 128 * t:128 * t + 128],
                                   wout_sb[ec][:, 512 * dg:512 * dg + 512],
                                   start=(ec == 0), stop=False)
                ins = e.matmul(pb[dg][:, 0:512], onesr[0:1, 0:128],
                               bout_sb[0:1, 512 * dg:512 * dg + 512], start=False, stop=True)
            return ins
        p.emit("tensor", op_mms, wait=wmm, inc=[(s_mm, 1)])
        opst[t] = s_mm.v

    def p5_evac(t):
        def op_evac(e, t=t):
            ins = None
            for dg in range(4):
                ins = e.tensor_copy(y1[t % 2][:, 512 * dg:512 * dg + 512], pb[dg][:, 0:512])
            return ins
        p.emit("vector", op_evac, wait=[(s_mm, opst[t])], inc=[(s_ve, 1)])
        p5_evst[t] = s_ve.v

    def p5_ladder(t):
        sb = 16 * (t % 2)
        yy = y1[t % 2]
        sc = scr5p[t % 2]
        x_sem, x_stamp = x2add[("sem", t)]
        p.emit("vector", lambda e, yy=yy, sb=sb: e.reduce_sum(stat[:, sb:sb + 1], yy[:, :], axis=AX),
               wait=[(s_ve, p5_evst[t])])
        p.emit("scalar", lambda e, yy=yy, sc=sc: e.square(sc[:, :], yy[:, :]),
               wait=[(s_ve, p5_evst[t])], inc=[(s_se, 1)])
        p.emit("vector", lambda e, sb=sb, sc=sc: e.reduce_sum(stat[:, sb + 1:sb + 2], sc[:, :], axis=AX),
               wait=[(s_se, s_se.v)], inc=[(s_ve, 1)])
        p.emit("vector", lambda e, sb=sb: e.tensor_scalar_mul(stat[:, sb + 2:sb + 3], stat[:, sb:sb + 1], 1.0 / D),
               wait=[(s_ve, s_ve.v)], inc=[(s_ve, 1)])
        p.emit("vector", lambda e, sb=sb: e.tensor_scalar_mul(stat[:, sb + 3:sb + 4], stat[:, sb + 1:sb + 2], 1.0 / D),
               wait=[(s_ve, s_ve.v)], inc=[(s_ve, 1)])
        p.emit("vector", lambda e, sb=sb: e.scalar_tensor_tensor(out=stat[:, sb + 4:sb + 5], in0=stat[:, sb + 2:sb + 3],
                                                                 scalar=1.0, in1=stat[:, sb + 2:sb + 3],
                                                                 op0=ALU.mult, op1=ALU.mult),
               wait=[(s_ve, s_ve.v)], inc=[(s_ve, 1)])
        p.emit("vector", lambda e, sb=sb: e.tensor_sub(stat[:, sb + 5:sb + 6], stat[:, sb + 3:sb + 4],
                                                       stat[:, sb + 4:sb + 5]),
               wait=[(s_ve, s_ve.v)], inc=[(s_ve, 1)])
        p.emit("scalar", lambda e, sb=sb: e.activation(out=stat[:, sb + 7:sb + 8], in_=stat[:, sb + 5:sb + 6],
                                                       func=ACTF.Ln, bias=consts[:, C_EPS:C_EPS + 1],
                                                       scale=consts[:, C_ONE:C_ONE + 1]),
               wait=[(s_ve, s_ve.v)], inc=[(s_se, 1)])
        p.emit("scalar", lambda e, sb=sb: e.activation(out=stat[:, sb + 6:sb + 7], in_=stat[:, sb + 7:sb + 8],
                                                       func=ACTF.Exp, bias=consts[:, C_ZERO:C_ZERO + 1],
                                                       scale=consts[:, C_NHALF:C_NHALF + 1]),
               wait=[(s_se, s_se.v)], inc=[(s_se, 1)])
        p.emit("vector", lambda e, yy=yy, sb=sb, sc=sc: e.tensor_scalar(
            out=sc[:, :], in0=yy[:, :], scalar1=stat[:, sb + 2:sb + 3], scalar2=stat[:, sb + 6:sb + 7],
            op0=ALU.subtract, op1=ALU.mult),
               wait=[(s_se, s_se.v)], inc=[(s_ve, 1)])
        p.emit("vector", lambda e, t=t, sc=sc: e.tensor_add(x2[t][:, :], sc[:, :], xb5[t % 3][:, :]),
               wait=[(x_sem, x_stamp)], inc=[(s_ve, 1)])
        x2add[t] = s_ve.v
        p.emit("vector", lambda e, t=t, sb=sb: e.reduce_sum(stat[:, sb + 8:sb + 9], x2[t][:, :], axis=AX),
               wait=[(s_ve, s_ve.v)])
        p.emit("scalar", lambda e, t=t, sc=sc: e.square(sc[:, :], x2[t][:, :]),
               wait=[(s_ve, x2add[t])], inc=[(s_se, 1)])
        p.emit("vector", lambda e, sb=sb, sc=sc: e.reduce_sum(stat[:, sb + 9:sb + 10], sc[:, :], axis=AX),
               wait=[(s_se, s_se.v)], inc=[(s_ve, 1)])
        p.emit("vector", lambda e, sb=sb: e.tensor_scalar_mul(stat[:, sb + 10:sb + 11], stat[:, sb + 8:sb + 9], 1.0 / D),
               wait=[(s_ve, s_ve.v)], inc=[(s_ve, 1)])
        p.emit("vector", lambda e, sb=sb: e.tensor_scalar_mul(stat[:, sb + 11:sb + 12], stat[:, sb + 9:sb + 10], 1.0 / D),
               wait=[(s_ve, s_ve.v)], inc=[(s_ve, 1)])
        p.emit("vector", lambda e, sb=sb: e.scalar_tensor_tensor(out=stat[:, sb + 12:sb + 13],
                                                                 in0=stat[:, sb + 10:sb + 11], scalar=1.0,
                                                                 in1=stat[:, sb + 10:sb + 11],
                                                                 op0=ALU.mult, op1=ALU.mult),
               wait=[(s_ve, s_ve.v)], inc=[(s_ve, 1)])
        p.emit("vector", lambda e, sb=sb: e.tensor_sub(stat[:, sb + 13:sb + 14], stat[:, sb + 11:sb + 12],
                                                       stat[:, sb + 12:sb + 13]),
               wait=[(s_ve, s_ve.v)], inc=[(s_ve, 1)])
        p.emit("scalar", lambda e, sb=sb: e.activation(out=stat[:, sb + 15:sb + 16], in_=stat[:, sb + 13:sb + 14],
                                                       func=ACTF.Ln, bias=consts[:, C_EPS:C_EPS + 1],
                                                       scale=consts[:, C_ONE:C_ONE + 1]),
               wait=[(s_ve, s_ve.v)], inc=[(s_se, 1)])
        p.emit("scalar", lambda e, sb=sb: e.activation(out=stat[:, sb + 14:sb + 15], in_=stat[:, sb + 15:sb + 16],
                                                       func=ACTF.Exp, bias=consts[:, C_ZERO:C_ZERO + 1],
                                                       scale=consts[:, C_NHALF:C_NHALF + 1]),
               wait=[(s_se, s_se.v)], inc=[(s_se, 1)])
        wl3 = [(s_se, s_se.v)]
        if t >= 2:
            wl3.append((s_mm, l3st[t - 2]))
        p.emit("vector", lambda e, t=t, sb=sb: e.tensor_scalar(out=l3t[t % 2][:, :], in0=x2[t][:, :],
                                                               scalar1=stat[:, sb + 10:sb + 11],
                                                               scalar2=stat[:, sb + 14:sb + 15],
                                                               op0=ALU.subtract, op1=ALU.mult),
               wait=wl3, inc=[(s_ve, 1)])
        l3_ve = s_ve.v
        for fq in range(4):
            def trans(e, t=t, fq=fq):
                ins = None
                for i in range(4):
                    dc = 4 * fq + i
                    ins = e.transpose(ptr[:, 128 * i:128 * i + 128],
                                      l3t[t % 2][:, 128 * dc:128 * dc + 128], ident[:, :])
                return ins
            wev = [(s_ve, l3_ve), (s_se, sestamp["tr"])]
            if dbg == "fd" and t == 0 and fq == 0:
                wev.append((s_dbg, AO_DUMP))
            p.emit("tensor", trans, wait=wev, inc=[(s_mm, 1)])

            def evac(e, t=t, fq=fq):
                ins = None
                for i in range(4):
                    dc = 4 * fq + i
                    ins = e.copy(ln3_fm[dc][:, 128 * t:128 * t + 128], ptr[:, 128 * i:128 * i + 128])
                return ins
            p.emit("scalar", evac, wait=[(s_mm, s_mm.v)], inc=[(s_se, 1)])
            sestamp["tr"] = s_se.v
        l3st[t] = s_mm.v
        if dbg == "p5a" and t == 0:
            pass

    for t in range(NT_OWN + 2):
        if 1 <= t <= NT_OWN:
            p5_evac(t - 1)
        if t < NT_OWN:
            p5_mm(t)
        if t >= 2:
            p5_ladder(t - 2)
    ln3_se = s_se.v
    p.pour(nc, "p5")
    if dbg == "p5a":
        p = Plan()
        p.emit("sync", lambda e: e.nop(), wait=[(s_dbg, s_dbg.v), (s_thr, 16)])
        p.pour(nc, "dbgf")
        return nc

    if dbg == "p5":
        p = Plan()
        for t in range(NT_OWN):
            p.emit("sync", lambda e, t=t: e.dma_start(out=dbg_outs["x2"][128 * t:128 * t + 128, :],
                                                      in_=x2[t][:, :]), inc=[(s_dbg, 16)])
        for dc in range(ND):
            p.emit("sync", lambda e, dc=dc: e.dma_start(out=dbg_outs["ln3"][128 * dc:128 * dc + 128, :],
                                                        in_=ln3_fm[dc][:, :]), inc=[(s_dbg, 16)])
        p.emit("sync", lambda e: e.nop(), wait=[(s_dbg, s_dbg.v), (s_thr, 16)])
        p.pour(nc, "dbgf")
        return nc

    stk5.close()

    stk6 = ExitStack()
    E6 = lambda name, shape, dt: stk6.enter_context(nc.sbuf_tensor(name, shape, dt))
    w3t = [E6(f"w3t{i}", [128, 2048], F16) for i in range(2 * YBLK)]
    hid = [E6(f"hid{i}", [128, TOK_OWN], F16) for i in range(8)]
    sgt = E6("sgt", [128, TOK_OWN], F16)
    tscr = E6("tscr", [128, 2048], F16)

    yt = ln1

    def w12slot(wgt, dc):
        base = 8 + 4 * wgt + dc // 4
        c0 = 512 * (dc % 4)
        return ln1[base][:, c0:c0 + 512]

    p = Plan()
    NSB = NHC // 2
    NBLK = NHC // YBLK
    GB_ALL = [(sb_, sb_.v) for sb_ in s_gbp]
    w12_fw, w3_fw = {}, {}
    tern12_ve, tern3_ve = {}, {}
    mm12_done, mm3_done = {}, {}
    sig_done, hid_done = {}, {}
    mm3_evac = {}

    def stream_w12_dma(pi):
        sfw = s_fw12[pi % 2]
        fw = list(GB_ALL) + [(s_thr, 16)]
        if pi >= 1:
            fw.append((s_mm, mm12_done[pi - 1]))
        first = True
        for wgt, src in ((0, w1f), (1, w2f)):
            for dc in range(ND):
                p.emit("sync", lambda e, src=src, dc=dc, pi=pi, wgt=wgt: e.dma_start(
                    out=w12slot(wgt, dc),
                    in_=src[512 * pi:512 * pi + 512, 128 * dc:128 * dc + 128], transpose=True),
                       wait=(fw if first else []), inc=[(sfw, 16)])
                first = False
        w12_fw[pi] = (sfw, sfw.v)

    def tern_w12(pi):
        def tern(e):
            ins = None
            for wgt in range(2):
                for fr in range(4):
                    tgt = ln1[8 + 4 * wgt + fr][:, 0:2048]
                    ins = e.tensor_scalar(out=tscr[:, 0:2048], in0=tgt, scalar1=thr[:, 4 + wgt:5 + wgt],
                                          scalar2=1.0, op0=ALU.is_gt, op1=ALU.subtract)
                    ins = e.scalar_tensor_tensor(out=tgt, in0=tgt, scalar=thr[:, wgt:wgt + 1],
                                                 in1=tscr[:, 0:2048], op0=ALU.is_ge, op1=ALU.add)
            return ins
        p.emit("vector", tern, wait=[w12_fw[pi], (s_thr, 16)], inc=[(s_ve, 1)])
        tern12_ve[pi] = s_ve.v

    def stream_w3(bi):
        par = bi % 2
        sfw = s_fw3[par]
        first_wait = list(GB_ALL) + [(s_thr, 16)]
        if bi >= 2:
            first_wait.append((s_ve, mm3_evac.get("last", 0)))
            first_wait.append((s_mm, mm3_done[bi - 2]))
        first = True
        for i in range(YBLK):
            p.emit("sync", lambda e, i=i, bi=bi, par=par: e.dma_start(
                out=w3t[YBLK * par + i][:, :],
                in_=w3f[0:D, 128 * (YBLK * bi + i):128 * (YBLK * bi + i) + 128], transpose=True),
                   wait=(first_wait if first else []), inc=[(sfw, 16)])
            first = False
        w3_fw[bi] = (sfw, sfw.v)

        def tern(e, par=par):
            ins = None
            for i in range(YBLK):
                for c0 in range(0, 2048, 512):
                    tgt = w3t[YBLK * par + i][:, c0:c0 + 512]
                    ins = e.tensor_scalar(out=tscr[:, 0:512], in0=tgt, scalar1=thr[:, 6:7],
                                          scalar2=1.0, op0=ALU.is_gt, op1=ALU.subtract)
                    ins = e.scalar_tensor_tensor(out=tgt, in0=tgt, scalar=thr[:, 2:3],
                                                 in1=tscr[:, 0:512], op0=ALU.is_ge, op1=ALU.add)
            return ins
        p.emit("vector", tern, wait=[w3_fw[bi], (s_thr, 16)], inc=[(s_ve, 1)])
        tern3_ve[bi] = s_ve.v

    stream_w12_dma(0)
    tern_w12(0)
    stream_w3(0)
    for bi in range(NBLK):
        par = bi % 2
        for i in range(YBLK):
            hc = YBLK * bi + i
            for tg in range(2):
                wml = [(s_ve, tern12_ve[bi])]
                if bi == 0 and i == 0 and tg == 0:
                    wml.append((s_se, ln3_se))
                if hc >= 1:
                    wml.append((s_se, sig_done.get((hc - 1, tg), 0)))
                    wml.append((s_ve, hid_done.get((hc - 1, tg), 0)))

                def mm12(e, hc=hc, tg=tg):
                    ins = None
                    for wgt in range(2):
                        bank = pb[2 * wgt + tg]
                        for dc in range(ND):
                            lt = w12slot(wgt, dc)[:, 128 * (hc % 4):128 * (hc % 4) + 128]
                            ins = e.matmul(bank[:, 0:512], lt,
                                           ln3_fm[dc][:, 512 * tg:512 * tg + 512],
                                           start=(dc == 0), stop=(dc == ND - 1))
                    return ins
                p.emit("tensor", mm12, wait=wml, inc=[(s_mm, 1)])
                mm12_done[bi] = s_mm.v
                wsg = [(s_mm, s_mm.v)]
                if hc >= 1:
                    wsg.append((s_ve, hid_done.get((hc - 1, tg), 0)))
                p.emit("scalar", lambda e, tg=tg: e.activation(
                    out=sgt[:, 512 * tg:512 * tg + 512], in_=pb[tg][:, 0:512], func=ACTF.Sigmoid),
                       wait=wsg, inc=[(s_se, 1)])
                sig_done[(hc, tg)] = s_se.v
                whid = [(s_se, s_se.v)]
                if hc >= 8:
                    whid.append((s_mm, mm3_done[bi - 2]))
                p.emit("vector", lambda e, hc=hc, tg=tg: e.tensor_mul(
                    hid[hc % 8][:, 512 * tg:512 * tg + 512],
                    pb[2 + tg][:, 0:512], sgt[:, 512 * tg:512 * tg + 512]),
                       wait=whid, inc=[(s_ve, 1)])
                hid_done[(hc, tg)] = s_ve.v
        hid_all = s_ve.v
        if bi + 1 < NBLK:
            stream_w12_dma(bi + 1)
        for t in range(NT_OWN):
            for dg in range(4):
                bank = pb[2 + ((t * 4 + dg) % 4)]

                def mm3(e, t=t, dg=dg, bi=bi, par=par, bank=bank):
                    ins = None
                    for i in range(YBLK):
                        ins = e.matmul(bank[:, 0:512],
                                       hid[(YBLK * bi + i) % 8][:, 128 * t:128 * t + 128],
                                       w3t[YBLK * par + i][:, 512 * dg:512 * dg + 512],
                                       start=(i == 0), stop=(i == YBLK - 1))
                    return ins
                wm3 = [(s_ve, hid_all), (s_ve, tern3_ve[bi])]
                prev = mm3_evac.get((t * 4 + dg) % 4)
                if prev:
                    wm3.append((s_ve, prev))
                p.emit("tensor", mm3, wait=wm3, inc=[(s_mm, 1)])
                mm3_done[bi] = s_mm.v

                if bi == 0:
                    p.emit("vector", lambda e, t=t, dg=dg, bank=bank: e.tensor_copy(
                        yt[t][:, 512 * dg:512 * dg + 512], bank[:, 0:512]),
                           wait=[(s_mm, s_mm.v)], inc=[(s_ve, 1)])
                else:
                    p.emit("vector", lambda e, t=t, dg=dg, bank=bank: e.tensor_add(
                        yt[t][:, 512 * dg:512 * dg + 512], yt[t][:, 512 * dg:512 * dg + 512],
                        bank[:, 0:512]),
                           wait=[(s_mm, s_mm.v)], inc=[(s_ve, 1)])
                mm3_evac[(t * 4 + dg) % 4] = s_ve.v
                mm3_evac["last"] = s_ve.v
        y_ve = s_ve.v
        if bi + 1 < NBLK:
            tern_w12(bi + 1)
            stream_w3(bi + 1)
    p.pour(nc, "p6")

    if dbg == "p6":
        p = Plan()
        for t in range(NT_OWN):
            p.emit("sync", lambda e, t=t: e.dma_start(out=dbg_outs["y"][128 * t:128 * t + 128, :],
                                                      in_=yt[t][:, 0:D]),
                   wait=([(s_ve, y_ve)] if t == 0 else []), inc=[(s_dbg, 16)])
        p.emit("sync", lambda e: e.nop(), wait=[(s_dbg, s_dbg.v)])
        p.pour(nc, "dbgf")
        return nc

    stk6.close()
    stk7 = ExitStack()
    E7 = lambda name, shape, dt: stk7.enter_context(nc.sbuf_tensor(name, shape, dt))
    obuf = [E7(f"obuf{i}", [128, D], F32) for i in range(2)]

    p = Plan()
    if dbg == "fd":
        for t in range(NT_OWN):
            p.emit("sync", lambda e, t=t: e.dma_start(out=dbg_outs["x2d"][128 * t:128 * t + 128, :],
                                                      in_=x2[t][:, :]), inc=[(s_dbg, 16)])
            p.emit("sync", lambda e, t=t: e.dma_start(out=dbg_outs["yd"][128 * t:128 * t + 128, :],
                                                      in_=yt[t][:, 0:D]),
                   wait=([(s_ve, y_ve)] if t == 0 else []), inc=[(s_dbg, 16)])
    add7 = {}
    r7 = {}

    def p7_s1(t):
        sb = 16 * (t % 2)
        sc = scr5p[t % 2]
        p.emit("vector", lambda e, t=t, sb=sb: e.reduce_sum(stat[:, sb:sb + 1], yt[t][:, 0:D], axis=AX),
               wait=[(s_ve, y_ve)])
        wsq7 = [(s_ve, y_ve)]
        if t >= 2:
            wsq7.append((s_ve, add7[t - 2]))
        p.emit("scalar", lambda e, t=t, sc=sc: e.square(sc[:, :], yt[t][:, 0:D]),
               wait=wsq7, inc=[(s_se, 1)])
        p.emit("vector", lambda e, sb=sb, sc=sc: e.reduce_sum(stat[:, sb + 1:sb + 2], sc[:, :], axis=AX),
               wait=[(s_se, s_se.v)], inc=[(s_ve, 1)])
        p.emit("vector", lambda e, sb=sb: e.tensor_scalar_mul(stat[:, sb + 2:sb + 3], stat[:, sb:sb + 1], 1.0 / D),
               wait=[(s_ve, s_ve.v)], inc=[(s_ve, 1)])
        p.emit("vector", lambda e, sb=sb: e.tensor_scalar_mul(stat[:, sb + 3:sb + 4], stat[:, sb + 1:sb + 2], 1.0 / D),
               wait=[(s_ve, s_ve.v)], inc=[(s_ve, 1)])
        p.emit("vector", lambda e, sb=sb: e.scalar_tensor_tensor(out=stat[:, sb + 4:sb + 5], in0=stat[:, sb + 2:sb + 3],
                                                                 scalar=1.0, in1=stat[:, sb + 2:sb + 3],
                                                                 op0=ALU.mult, op1=ALU.mult),
               wait=[(s_ve, s_ve.v)], inc=[(s_ve, 1)])
        p.emit("vector", lambda e, sb=sb: e.tensor_sub(stat[:, sb + 5:sb + 6], stat[:, sb + 3:sb + 4],
                                                       stat[:, sb + 4:sb + 5]),
               wait=[(s_ve, s_ve.v)], inc=[(s_ve, 1)])
        p.emit("scalar", lambda e, sb=sb: e.activation(out=stat[:, sb + 7:sb + 8], in_=stat[:, sb + 5:sb + 6],
                                                       func=ACTF.Ln, bias=consts[:, C_EPS:C_EPS + 1],
                                                       scale=consts[:, C_ONE:C_ONE + 1]),
               wait=[(s_ve, s_ve.v)], inc=[(s_se, 1)])
        p.emit("scalar", lambda e, sb=sb: e.activation(out=stat[:, sb + 6:sb + 7], in_=stat[:, sb + 7:sb + 8],
                                                       func=ACTF.Exp, bias=consts[:, C_ZERO:C_ZERO + 1],
                                                       scale=consts[:, C_NHALF:C_NHALF + 1]),
               wait=[(s_se, s_se.v)], inc=[(s_se, 1)])
        r7[t] = s_se.v

    def p7_s2(t):
        sb = 16 * (t % 2)
        sc = scr5p[t % 2]
        sst = s_stp[t % 2]
        p.emit("vector", lambda e, t=t, sb=sb, sc=sc: e.tensor_scalar(
            out=sc[:, :], in0=yt[t][:, 0:D], scalar1=stat[:, sb + 2:sb + 3],
            scalar2=stat[:, sb + 6:sb + 7], op0=ALU.subtract, op1=ALU.mult),
               wait=[(s_se, r7[t])], inc=[(s_ve, 1)])
        wadd = [(s_ve, s_ve.v)]
        if t >= 2:
            wadd.append((sst, sst.v))
        p.emit("vector", lambda e, t=t, sc=sc: e.tensor_add(obuf[t % 2][:, :], sc[:, :], x2[t][:, :]),
               wait=wadd, inc=[(s_ve, 1)])
        add7[t] = s_ve.v
        p.emit("sync", lambda e, t=t: e.dma_start(out=yout[128 * t:128 * t + 128, :], in_=obuf[t % 2][:, :]),
               wait=[(s_ve, s_ve.v)], inc=[(sst, 16)])

    for t in range(NT_OWN + 1):
        if t < NT_OWN:
            p7_s1(t)
        if t >= 1:
            p7_s2(t - 1)
    p.emit("sync", lambda e: e.nop(), wait=[(s_stp[0], s_stp[0].v), (s_stp[1], s_stp[1].v)])
    p.pour(nc, "p7")
    stk7.close()
    stk5o.close()
    pstk.close()
    return nc



def _host_inputs(L0_attn_x, L0_attn_Wqkv, L0_attn_bqkv, L0_attn_Wout, L0_attn_bout,
                 L0_ffn_w1, L0_ffn_w2, L0_ffn_w3):
    x = np.asarray(L0_attn_x, dtype=np.float32)
    Wqkv = np.ascontiguousarray(np.asarray(L0_attn_Wqkv, dtype=np.float32))
    bq = np.asarray(L0_attn_bqkv, dtype=np.float32)
    Wout = np.ascontiguousarray(np.asarray(L0_attn_Wout, dtype=np.float32))
    bo = np.asarray(L0_attn_bout, dtype=np.float32)
    w1 = np.ascontiguousarray(np.asarray(L0_ffn_w1, dtype=np.float32))
    w2 = np.ascontiguousarray(np.asarray(L0_ffn_w2, dtype=np.float32))
    w3 = np.ascontiguousarray(np.asarray(L0_ffn_w3, dtype=np.float32))

    perm = []
    for g in range(NG):
        for part in range(3):
            base = part * D + 512 * g
            perm += list(range(base, base + 512))
    wqkv_r = np.ascontiguousarray(Wqkv[:, perm])
    bqkv_r = np.ascontiguousarray(bq[perm][None, :])

    inv = (1.0 / 10000.0 ** (np.arange(0, ROT, 2, dtype=np.float32) / ROT)).astype(np.float32)

    tri = (np.arange(128)[:, None] <= np.arange(128)[None, :]).astype(np.float32)
    mb = np.zeros((128, 896), dtype=np.float32)
    mb[:, 384:512] = tri
    mb[:, 512:896] = 1.0
    mband = mb.astype(np.float16)

    ident = np.eye(128, dtype=np.float16)

    consts = np.zeros((128, NCONST), dtype=np.float32)
    consts[:, C_EXPSC] = 0.125
    consts[:, C_EPS] = 1e-5
    consts[:, C_NHALF] = -0.5
    consts[:, C_INVD] = 1.0 / D
    consts[:, C_THRSC] = 0.5 / (HFF * D)
    consts[:, C_THRB] = 0.5e-8
    consts[:, C_ONE] = 1.0
    consts[:, C_NEG1] = -1.0
    consts[:, C_ZERO] = 0.0

    in_maps = []
    for c in range(NCORE):
        b, half = c // 2, c % 2
        own = slice(half * TOK_OWN, (half + 1) * TOK_OWN)
        oth = slice((1 - half) * TOK_OWN, (2 - half) * TOK_OWN)
        xr = np.ascontiguousarray(np.concatenate([x[b, own], x[b, oth]], axis=0))
        pos = np.concatenate([np.arange(half * TOK_OWN, (half + 1) * TOK_OWN),
                              np.arange((1 - half) * TOK_OWN, (2 - half) * TOK_OWN)]).astype(np.float32)
        fr = np.outer(pos, inv)
        cosv, sinv = np.cos(fr).astype(np.float32), np.sin(fr).astype(np.float32)
        c1h = np.concatenate([cosv, cosv], axis=1)
        c2h = np.concatenate([-sinv, sinv], axis=1)
        c1 = np.ascontiguousarray(np.tile(c1h, (1, GH)).astype(np.float16))
        c2 = np.ascontiguousarray(np.tile(c2h, (1, GH)).astype(np.float16))

        btblv = np.full((128, 32), -2.0, dtype=np.float32)
        if half == 0:
            for st in range(2):
                for j in range(8, 16):
                    btblv[:, st * 16 + j] = -1250.0

        in_maps.append({
            "xin": xr, "wqkv": wqkv_r, "bqkv": bqkv_r, "wout": Wout, "bout": bo[None, :],
            "w1": w1, "w2": w2, "w3": w3, "c1": c1, "c2": c2, "btbl": btblv,
            "mband": mband, "ident": ident, "consts": consts,
        })
    return in_maps


def kernel(**inputs):
    in_maps = _host_inputs(**inputs)
    nc = build()
    res = run_bass_kernel_spmd(nc, in_maps, core_ids=list(range(NCORE)))
    out = np.zeros((B, S, D), dtype=np.float32)
    for c in range(NCORE):
        b, half = c // 2, c % 2
        out[b, half * TOK_OWN:(half + 1) * TOK_OWN] = res.results[c]["yout"]
    return out
